# version 02
# baseline (speedup 1.0000x reference)
import numpy as np
import ml_dtypes
import concourse.bass as bass
import concourse.mybir as mybir
from concourse.bass_utils import run_bass_kernel_spmd

F32 = mybir.dt.float32
BF16 = mybir.dt.bfloat16
AF = mybir.ActivationFunctionType
ALU = mybir.AluOpType
AX = mybir.AxisListType
BF = ml_dtypes.bfloat16

ENG = ("sync", "gpsimd", "vector", "scalar", "tensor")
D = 1024
TB = 2048
NB = 2
TOK = TB * NB
NCORES = 8
H = 16
DH = 64
DECAY = np.float32(0.99)
NEG = -1e30


class P:
    def __init__(self, nc):
        self.nc = nc
        self.streams = {e: [] for e in ENG}
        self._sems = {}
        self._ctx = []
        self._pending = {e: [] for e in ENG}
        self._uid = 0

    def uid(self):
        self._uid += 1
        return self._uid

    def sem(self, name):
        if name not in self._sems:
            cm = self.nc.semaphore(name)
            h = cm.__enter__()
            self._ctx.append(cm)
            self._sems[name] = [h, 0]
        return self._sems[name]

    def n(self, name):
        return self.sem(name)[1]

    def op(self, eng, fn, wait=(), dma=False):
        s = self.sem(eng)
        delta = 16 if dma else 1
        s[1] += delta
        w = list(self._pending[eng]) + list(wait)
        self._pending[eng] = []
        self.streams[eng].append((w, fn, [(eng, delta)]))
        return s[1]

    def dma(self, q, out, in_, wait=(), slow=False):
        if slow:
            return self.op(q, lambda e: e.dma_start(
                out=out, in_=in_, allow_slow_non_contiguous=True), wait, dma=True)
        return self.op(q, lambda e: e.dma_start(out=out, in_=in_), wait, dma=True)

    def barrier(self):
        snap = [(e, self._sems[e][1]) for e in ENG
                if e in self._sems and self._sems[e][1] > 0]
        for e in ENG:
            self._pending[e] = list(snap)

    def finalize(self):
        sems = self._sems
        self_chain = ("vector", "scalar", "gpsimd")

        def mk(name):
            items = self.streams[name]
            chain = name in self_chain

            def run(eng):
                prev = 0
                for waits, fn, incs in items:
                    m = {}
                    for sname, val in waits:
                        if val > 0:
                            m[sname] = max(m.get(sname, 0), val)
                    if chain and prev > 0:
                        m[name] = max(m.get(name, 0), prev)
                    for sname, val in m.items():
                        eng.wait_ge(sems[sname][0], val)
                    ins = fn(eng)
                    for sname, delta in incs:
                        ins.then_inc(sems[sname][0], delta)
                        if sname == name:
                            prev += delta
            return run

        with self.nc.Block() as block:
            for e in ENG:
                if self.streams[e]:
                    getattr(block, e)(mk(e))


class Arena:
    def __init__(self, nc, p):
        self.nc = nc
        self.p = p
        self.cms = []

    def sb(self, name, shape, dt=F32):
        cm = self.nc.sbuf_tensor(f"{name}_{self.p.uid()}", list(shape), dt)
        h = cm.__enter__()
        self.cms.append(cm)
        return h

    def ps(self, name, shape, dt=F32):
        cm = self.nc.psum_tensor(f"{name}_{self.p.uid()}", list(shape), dt)
        h = cm.__enter__()
        self.cms.append(cm)
        return h

    def close(self):
        for cm in reversed(self.cms):
            cm.__exit__(None, None, None)
        self.cms = []


def _tile_lhsT(w):
    K, O = w.shape
    return np.ascontiguousarray(
        w.reshape(K // 128, 128, O // 128, 128).transpose(2, 1, 0, 3)
        .reshape(O // 128, 128, (K // 128) * 128))


def prep_host(inputs):
    c = {}
    f = np.float32

    c["w_qkv"] = _tile_lhsT(inputs["L0_attn_qkv_w"].astype(f))
    c["b_qkv_col"] = np.ascontiguousarray(
        inputs["L0_attn_qkv_b"].astype(f).reshape(24, 128).T)
    c["w_qkv_v"] = np.ascontiguousarray(inputs["L0_attn_qkv_w"][:, 2048:3072].astype(f))
    c["b_qkv_v_row"] = np.ascontiguousarray(
        inputs["L0_attn_qkv_b"][2048:3072].astype(f).reshape(1, 1024))
    c["w_l0proj"] = np.ascontiguousarray(np.concatenate(
        [inputs["L0_attn_w1w"], inputs["L0_attn_w2w"], inputs["L0_attn_r1w"],
         inputs["L0_attn_r2w"], inputs["L0_attn_memg_w"]], axis=1).astype(f))
    c["b_memg_tile"] = np.ascontiguousarray(
        np.tile(inputs["L0_attn_memg_b"].astype(f).reshape(1, 16), (128, 1)))
    c["memscale_tile"] = np.ascontiguousarray(
        np.tile(inputs["L0_attn_mem_scale"].astype(f).reshape(1, 16), (128, 1)))
    c["w_memv"] = _tile_lhsT(inputs["L0_attn_memv_w"].astype(f))
    c["b_memv_col"] = np.ascontiguousarray(
        inputs["L0_attn_memv_b"].astype(f).reshape(8, 128).T)
    c["w_l0out"] = _tile_lhsT(inputs["L0_attn_out_w"].astype(f))
    c["b_l0out_col"] = np.ascontiguousarray(
        inputs["L0_attn_out_b"].astype(f).reshape(8, 128).T)

    w12 = np.concatenate([inputs["L0_ffn_w1_w"].T, inputs["L0_ffn_w2_w"].T],
                         axis=1).astype(f)
    c["w_f12"] = _tile_lhsT(w12)
    c["s_f12_col"] = np.ascontiguousarray(np.concatenate(
        [inputs["L0_ffn_w1_s"], inputs["L0_ffn_w2_s"]]).astype(f).reshape(64, 128).T)
    c["w_fh1"] = _tile_lhsT(inputs["L0_ffn_h1_w"].T.astype(f))
    c["s_fh1_col"] = np.ascontiguousarray(
        inputs["L0_ffn_h1_s"].astype(f).reshape(32, 128).T)
    c["w_fh2"] = _tile_lhsT(inputs["L0_ffn_h2_w"].T.astype(f))
    c["s_fh2_col"] = np.ascontiguousarray(
        inputs["L0_ffn_h2_s"].astype(f).reshape(32, 128).T)
    c["w_f3"] = _tile_lhsT(inputs["L0_ffn_w3_w"].T.astype(f))
    c["s_f3_col"] = np.ascontiguousarray(
        inputs["L0_ffn_w3_s"].astype(f).reshape(8, 128).T)

    c["w_l1in"] = _tile_lhsT(inputs["L1_attn_w_in"].T.astype(f))
    c["w_l1in_v"] = np.ascontiguousarray(inputs["L1_attn_w_in"].T[:, 2048:3072].astype(f))
    c["w_l1out"] = _tile_lhsT(inputs["L1_attn_w_out"].T.astype(f))

    c["wg"] = np.ascontiguousarray(inputs["L1_ffn_Wg"].astype(f))
    W1 = inputs["L1_ffn_W1"].astype(f)
    W3 = inputs["L1_ffn_W3"].astype(f)
    W2 = inputs["L1_ffn_W2"].astype(f)
    c["w_e1"] = np.ascontiguousarray(
        W1.reshape(32, 8, 128, 4, 128).transpose(0, 2, 3, 1, 4)
        .reshape(32, 128, 4 * 8 * 128))
    c["w_e3"] = np.ascontiguousarray(
        W3.reshape(32, 8, 128, 4, 128).transpose(0, 2, 3, 1, 4)
        .reshape(32, 128, 4 * 8 * 128))
    c["w_e2"] = np.ascontiguousarray(
        W2.reshape(32, 4, 128, 8, 128).transpose(0, 2, 3, 1, 4)
        .reshape(32, 128, 8 * 4 * 128))
    c["w_s1"] = _tile_lhsT(inputs["L1_ffn_Ws1"].astype(f))
    c["w_s3"] = _tile_lhsT(inputs["L1_ffn_Ws3"].astype(f))
    c["w_s2"] = _tile_lhsT(inputs["L1_ffn_Ws2"].astype(f))

    c["ident_bf"] = np.eye(128, dtype=BF)
    c["ident_f32"] = np.eye(128, dtype=f)
    c["ones_row_bf"] = np.ones((1, 128), BF)
    i_ = np.arange(128)
    cm = np.zeros((128, 128), f)
    cm[i_[:, None] < i_[None, :]] = NEG
    c["cmask"] = cm
    c["cmaskT"] = np.ascontiguousarray(cm.T)
    E5 = np.zeros((128, 640), f)
    for j in range(5):
        delta = i_[:, None] - i_[None, :] + (4 - j) * 128
        Et = delta.astype(f)
        Et[(delta < 0) | (delta > 512)] = NEG
        E5[:, j * 128:(j + 1) * 128] = Et
    c["E5"] = E5
    E5T = np.zeros((128, 640), f)
    for d in range(5):
        E5T[:, d * 128:(d + 1) * 128] = E5[:, (4 - d) * 128:(5 - d) * 128].T
    c["E5T"] = E5T
    c["pfx_bf"] = np.triu(np.ones((128, 128), f), 1).astype(BF)
    c["wdn_col"] = (DECAY ** (-i_.astype(f))).reshape(128, 1).astype(f)
    c["gcol_bf"] = (DECAY ** (128 - i_.astype(f))).reshape(128, 1).astype(BF)
    c["tloc_col"] = (DECAY ** (i_.astype(f))).reshape(128, 1).astype(f)
    return c


WSPEC = {
    "w_qkv": F32, "b_qkv_col": F32, "w_qkv_v": F32, "b_qkv_v_row": F32,
    "w_l0proj": F32, "b_memg_tile": F32, "memscale_tile": F32,
    "w_memv": F32, "b_memv_col": F32, "w_l0out": F32, "b_l0out_col": F32,
    "w_f12": F32, "s_f12_col": F32, "w_fh1": F32, "s_fh1_col": F32,
    "w_fh2": F32, "s_fh2_col": F32, "w_f3": F32, "s_f3_col": F32,
    "w_l1in": F32, "w_l1in_v": F32, "w_l1out": F32,
    "wg": F32, "w_e1": F32, "w_e3": F32, "w_e2": F32,
    "w_s1": F32, "w_s3": F32, "w_s2": F32,
    "ident_bf": BF16, "ident_f32": F32, "ones_row_bf": BF16,
    "cmask": F32, "cmaskT": F32, "E5": F32, "E5T": F32, "pfx_bf": BF16,
    "wdn_col": F32, "gcol_bf": BF16, "tloc_col": F32,
}

PAIRS = [(0, 1), (0, 2), (0, 3), (1, 2), (1, 3), (2, 3)]
SIGN6 = [1.0, -1.0, 1.0, 1.0, -1.0, 1.0]


class K:
    def __init__(self, nc, host_arrays):
        self.nc = nc
        self.p = P(nc)
        self.t = {}
        self.c = {}
        self.host = host_arrays
        self.persist = Arena(nc, self.p)
        self.ext = []

    def dram(self, name, shape, dt, kind="Internal"):
        h = self.nc.dram_tensor(name, list(shape), dt, kind=kind)
        self.t[name] = h.ap()
        if kind == "ExternalInput" and name in self.host:
            self.ext.append(name)
        return self.t[name]

    def declare_weights(self, names):
        for nm in names:
            if nm in self.t:
                continue
            arr = self.host[nm]
            self.dram(nm, arr.shape, WSPEC[nm], kind="ExternalInput")

    def in_map(self, extra):
        m = {nm: self.host[nm] for nm in self.ext}
        m.update(extra)
        return m


def ph_consts(k, names):
    p = k.p
    for nm in names:
        to_bf = SMALL_SB[nm]
        if nm not in k.t:
            k.dram(nm, k.host[nm].shape, WSPEC[nm], kind="ExternalInput")
        ap = k.t[nm]
        shape = list(ap.shape)
        dt = BF16 if to_bf else WSPEC[nm]
        h = k.persist.sb("c_" + nm, shape, dt)
        q = "gpsimd" if (to_bf and WSPEC[nm] == F32) else "sync"
        p.dma(q, h[:], ap)
        k.c[nm] = h
    p.barrier()


def _ln_stats(p, st, src_ap, wait_src, war_wait):
    p.op("vector", lambda e: e.reduce_sum(out=st[:, 0:1], in_=src_ap, axis=AX.X),
         wait=wait_src + war_wait)
    a_ss = p.op("scalar", lambda e: e.activation(
        out=_ln_scr[0][:], in_=src_ap,
        func=AF.Square, accum_out=st[:, 1:2]), wait=wait_src + war_wait)
    p.op("vector", lambda e: e.tensor_scalar_mul(st[:, 2:3], st[:, 0:1], -1.0 / D))
    p.op("vector", lambda e: e.tensor_mul(st[:, 3:4], st[:, 2:3], st[:, 2:3]))
    p.op("vector", lambda e: e.scalar_tensor_tensor(
        out=st[:, 4:5], in0=st[:, 1:2], scalar=1.0 / D, in1=st[:, 3:4],
        op0=ALU.mult, op1=ALU.subtract), wait=[("scalar", a_ss)])
    v_var = p.op("vector", lambda e: e.tensor_scalar_add(st[:, 4:5], st[:, 4:5],
                                                         1e-5))
    a_sq = p.op("scalar", lambda e: e.activation(
        out=st[:, 5:6], in_=st[:, 4:5], func=AF.Sqrt),
        wait=[("vector", v_var)])
    v_inv = p.op("vector", lambda e: e.reciprocal(st[:, 6:7], st[:, 5:6]),
                 wait=[("scalar", a_sq)])
    return v_inv


_ln_scr = [None]


def ph_preln(k, x_src, xnT):
    nc, p = k.nc, k.p
    ar = Arena(nc, p)
    xt = [ar.sb("pl_x", [128, D]) for _ in range(2)]
    _ln_scr[0] = ar.sb("pl_scr", [128, D])
    st = [ar.sb("pl_st", [128, 8]) for _ in range(2)]
    xn = [ar.sb("pl_xn", [128, D], BF16) for _ in range(2)]
    stage = [ar.sb("pl_stg", [128, 8 * 512], BF16) for _ in range(2)]
    zpad = ar.sb("pl_zpad", [128, 8], BF16)
    pt = [ar.ps("pl_pt", [128, 128], BF16) for _ in range(2)]

    idb = k.c["ident_bf"]
    zv = p.op("vector", lambda e: e.memset(zpad[:], 0.0))
    for b in range(NB):
        dst = k.t_xnT_pad(xnT, b)
        p.dma("sync", dst, zpad[:, 0:8].unsqueeze(2), wait=[("vector", zv)],
              slow=True)

    xt_rd = [([], []) for _ in range(2)]
    st_rd = [0, 0]
    xn_rd = [0, 0]
    stage_dma = [0, 0]
    pt_evac = [0, 0]
    for i in range(TOK // 128):
        r = i % 2
        g = i % 4
        sg = (i // 4) % 2
        b = i // 16
        il = i % 16
        w = [("vector", xt_rd[r][0]), ("scalar", xt_rd[r][1])] if i >= 2 else []
        ld = p.dma("sync", xt[r][:], x_src[i * 128:(i + 1) * 128, :], wait=w)
        lw = [("sync", ld)]
        war = [("vector", st_rd[r])] if i >= 2 else []
        v_inv = _ln_stats(p, st[r], xt[r][:], lw, war)
        w = [("tensor", xn_rd[r])] if i >= 2 else []
        xn_done = p.op("vector", lambda e, r=r: e.tensor_scalar(
            out=xn[r][:], in0=xt[r][:], scalar1=st[r][:, 2:3],
            scalar2=st[r][:, 6:7], op0=ALU.add, op1=ALU.mult), wait=w)
        st_rd[r] = xn_done
        xt_rd[r] = (xn_done, p.n("scalar"))
        for cc in range(8):
            pr = (i * 8 + cc) % 2
            w = [("vector", xn_done)]
            if pt_evac[pr]:
                w.append(("scalar", pt_evac[pr]))
            mm = p.op("tensor", lambda e, r=r, cc=cc, pr=pr: e.transpose(
                out=pt[pr][:], in_=xn[r][:, cc * 128:(cc + 1) * 128],
                identity=idb[:]), wait=w)
            if cc == 7:
                xn_rd[r] = mm
            ew = [("tensor", mm)]
            if cc == 0 and g == 0 and stage_dma[sg]:
                ew.append(("sync", stage_dma[sg]))
            pt_evac[pr] = p.op("scalar", lambda e, pr=pr, cc=cc, sg=sg, g=g:
                e.copy(out=stage[sg][:, cc * 512 + g * 128: cc * 512 + g * 128 + 128],
                       in_=pt[pr][:]), wait=ew)
        if g == 3:
            dst = k.t_xnT_slab(xnT, b, (il - 3) * 128)
            stage_dma[sg] = p.dma(
                "sync", dst, stage[sg][:].rearrange("p (c t) -> p c t", c=8),
                wait=[("scalar", pt_evac[0]), ("scalar", pt_evac[1])])
    p.barrier()
    ar.close()


def _t_xnT_pad(xnT, b):
    return xnT[b].rearrange("(c p) T -> p c T", p=128)[:, :, 0:1]


def _t_xnT_slab(xnT, b, t0, width=512):
    return xnT[b].rearrange("(c p) T -> p c T", p=128)[:, :, 1 + t0: 1 + t0 + width]


K.t_xnT_pad = staticmethod(_t_xnT_pad)
K.t_xnT_slab = staticmethod(_t_xnT_slab)


def ph_post(k, y_sc, y_f32, x_src, x_dst):
    nc, p = k.nc, k.p
    ar = Arena(nc, p)
    ydt = F32 if y_f32 else BF16
    slab = [ar.sb("po_slab", [128, 8 * 512], ydt) for _ in range(2)]
    ytm = [ar.sb("po_ytm", [128, D], ydt) for _ in range(2)]
    xt = [ar.sb("po_x", [128, D]) for _ in range(2)]
    tmp = ar.sb("po_tmp", [128, D])
    _ln_scr[0] = ar.sb("po_scr", [128, D])
    st = [ar.sb("po_st", [128, 8]) for _ in range(2)]
    pt = [ar.ps("po_pt", [128, 128], ydt) for _ in range(2)]
    ident = k.c["ident_f32"] if y_f32 else k.c["ident_bf"]

    slab_load = [0, 0]
    slab_users = [0, 0]
    pt_evac = [0, 0]
    ytm_rd = [(0, 0) for _ in range(2)]
    st_rd = [0, 0]
    x_stores = [0, 0]
    for i in range(TOK // 128):
        r = i % 2
        g = i % 4
        sg = (i // 4) % 2
        if g == 0:
            w = [("tensor", slab_users[sg])] if slab_users[sg] else []
            src = y_sc.rearrange("(c p) T -> p c T", p=128)[:, :, i * 128:(i + 4) * 128]
            slab_load[sg] = p.dma("sync", slab[sg][:].rearrange(
                "p (c t) -> p c t", c=8), src, wait=w)
        for cc in range(8):
            pr = (i * 8 + cc) % 2
            w = [("sync", slab_load[sg])]
            if pt_evac[pr]:
                w.append(("scalar", pt_evac[pr]))
            if cc == 0 and i >= 2:
                w += [("vector", ytm_rd[r][0]), ("scalar", ytm_rd[r][1])]
            mm = p.op("tensor", lambda e, cc=cc, sg=sg, g=g, pr=pr: e.transpose(
                out=pt[pr][:], in_=slab[sg][:, cc * 512 + g * 128: cc * 512 + g * 128 + 128],
                identity=ident[:]), wait=w)
            if cc == 7:
                slab_users[sg] = mm
            pt_evac[pr] = p.op("scalar", lambda e, pr=pr, r=r, cc=cc: e.copy(
                out=ytm[r][:, cc * 128:(cc + 1) * 128], in_=pt[pr][:]),
                wait=[("tensor", mm)])
        ev = [("scalar", pt_evac[0]), ("scalar", pt_evac[1])]
        war = [("vector", st_rd[r])] if i >= 2 else []
        v_inv = _ln_stats(p, st[r], ytm[r][:], ev, war)
        act_rd = p.n("scalar")
        w = [("sync", x_stores[r])] if x_stores[r] else []
        x_load = p.dma("sync", xt[r][:], x_src[i * 128:(i + 1) * 128, :], wait=w)
        dv = p.op("vector", lambda e, r=r: e.tensor_scalar(
            out=tmp[:], in0=ytm[r][:], scalar1=st[r][:, 2:3], scalar2=st[r][:, 6:7],
            op0=ALU.add, op1=ALU.mult), wait=[("vector", v_inv)])
        st_rd[r] = dv
        ytm_rd[r] = (dv, act_rd)
        va = p.op("vector", lambda e, r=r: e.tensor_add(xt[r][:], xt[r][:], tmp[:]),
                  wait=[("sync", x_load)])
        x_stores[r] = p.dma("sync", x_dst[i * 128:(i + 1) * 128, :], xt[r][:],
                            wait=[("vector", va)])
    p.barrier()
    ar.close()


def _fm_chain_layer(k, ar, bufs, wname, sname, nk, oc_list, dst_fn, kind,
                    in_done, in_stride=TB, in_off=0):
    p = k.p
    inb, wt, sil, stage, ps = bufs
    wdram = k.t[wname]
    scol = k.c[sname] if sname is not None else None
    st = {"w_rd": [0, 0], "ps_free": [None, None], "stage_dma": [0, 0],
          "sil_rd": [0, 0]}
    first_mm_done = False
    for step, oc in enumerate(oc_list):
        r = step % 2
        par = r * 4
        w = [("tensor", st["w_rd"][r])] if st["w_rd"][r] else []
        wdma = p.dma("gpsimd", wt[r][:, 0:nk * 128], wdram[oc], wait=w)
        mw = [("gpsimd", wdma)]
        if not first_mm_done:
            mw += in_done
            first_mm_done = True
        if st["ps_free"][r] is not None:
            mw.append(st["ps_free"][r])
        for kc in range(nk):
            for t4 in range(4):
                p.op("tensor", lambda e, kc=kc, t4=t4, r=r, par=par, nk=nk:
                     e.matmul(out=ps[par + t4][:],
                              lhsT=wt[r][:, kc * 128:(kc + 1) * 128],
                              rhs=inb[:, kc * in_stride + in_off + t4 * 512:
                                      kc * in_stride + in_off + t4 * 512 + 512],
                              start=(kc == 0), stop=(kc == nk - 1)),
                     wait=(mw if kc == 0 and t4 == 0 else []))
        mm_end = p.n("tensor")
        st["w_rd"][r] = mm_end
        sc_ap = scol[:, oc:oc + 1] if scol is not None else None
        is_silu_step = (kind == "glu" and step % 2 == 0)
        is_glu2_step = (kind == "glu" and step % 2 == 1)
        rs = (step // 2) % 2
        evacs = []
        for t4 in range(4):
            mmw = [("tensor", mm_end - (3 - t4))]
            if is_silu_step:
                if t4 == 0 and st["sil_rd"][rs]:
                    mmw.append(("vector", st["sil_rd"][rs]))
                evacs.append(p.op("scalar", lambda e, t4=t4, par=par, rs=rs,
                                  sc_ap=sc_ap:
                    e.activation(out=sil[rs][:, t4 * 512:(t4 + 1) * 512],
                                 in_=ps[par + t4][:], func=AF.Silu,
                                 scale=sc_ap), wait=mmw))
            elif is_glu2_step:
                if t4 == 0 and st["stage_dma"][rs]:
                    mmw.append(("sync", st["stage_dma"][rs]))
                evacs.append(p.op("vector", lambda e, t4=t4, par=par, rs=rs,
                                  sc_ap=sc_ap:
                    e.scalar_tensor_tensor(
                        out=stage[rs][:, t4 * 512:(t4 + 1) * 512],
                        in0=ps[par + t4][:], scalar=sc_ap,
                        in1=sil[rs][:, t4 * 512:(t4 + 1) * 512],
                        op0=ALU.mult, op1=ALU.mult), wait=mmw))
            else:
                fn = AF.Gelu if kind == "gelu" else AF.Copy
                kw = {}
                if kind == "bias":
                    if scol is not None:
                        kw["bias"] = sc_ap
                        fn = AF.Identity
                else:
                    kw["scale"] = sc_ap
                sr = step % 2
                if t4 == 0 and st["stage_dma"][sr]:
                    mmw.append(("sync", st["stage_dma"][sr]))
                evacs.append(p.op("scalar", lambda e, t4=t4, par=par, sr=sr,
                                  fn=fn, kw=kw:
                    e.activation(out=stage[sr][:, t4 * 512:(t4 + 1) * 512],
                                 in_=ps[par + t4][:], func=fn,
                                 **kw), wait=mmw))
        eng = "vector" if is_glu2_step else "scalar"
        st["ps_free"][r] = (eng, evacs[-1])
        if is_silu_step:
            continue
        if is_glu2_step:
            st["sil_rd"][rs] = evacs[-1]
            c_out = oc - len(oc_list) // 2 if False else (step // 2)
            st["stage_dma"][rs] = p.dma("sync", dst_fn(step // 2), stage[rs][:],
                                        wait=[(eng, x) for x in evacs])
        else:
            sr = step % 2
            st["stage_dma"][sr] = p.dma("sync", dst_fn(oc), stage[sr][:],
                                        wait=[(eng, x) for x in evacs])
    p.barrier()


def _load_inb(k, inb, src_slabs):
    p = k.p
    cnt = 0
    for si, ap in enumerate(src_slabs):
        dst = inb[:, si * 8 * TB:(si + 1) * 8 * TB].rearrange(
            "p (c t) -> p c t", c=8)
        cnt = p.dma("sync", dst, ap)
    return [("sync", cnt)]


def _fm_rows(dram, c, b):
    return dram.rearrange("(c p) T -> p c T", p=128)[:, c, b * TB:(b + 1) * TB]


def ph_l0ffn(k, xnT, h_sc, h2_sc, h3_sc, y_sc):
    p = k.p
    for b in range(NB):
        ar = Arena(k.nc, p)
        inb = ar.sb("ff_in", [128, 32 * TB], BF16)
        wt = [ar.sb("ff_w", [128, 32 * 128], BF16) for _ in range(2)]
        sil = [ar.sb("ff_sil", [128, TB], BF16) for _ in range(2)]
        stage = [ar.sb("ff_stg", [128, TB], BF16) for _ in range(2)]
        ps = [ar.ps("ff_ps", [128, 512]) for _ in range(8)]
        bufs = (inb, wt, sil, stage, ps)

        ind = _load_inb(k, inb, [k.t_xnT_slab(xnT, b, 0, TB)])
        order = []
        for cpair in range(32):
            order += [cpair, 32 + cpair]
        _fm_chain_layer(k, ar, bufs, "w_f12", "s_f12_col", 8, order,
                        lambda c, b=b: _fm_rows(h_sc, c, b), "glu", ind)
        hb = h_sc.rearrange("(c p) T -> p c T", p=128)
        ind = _load_inb(k, inb, [
            hb[:, q * 8:(q + 1) * 8, b * TB:(b + 1) * TB] for q in range(4)])
        _fm_chain_layer(k, ar, bufs, "w_fh1", "s_fh1_col", 32, list(range(32)),
                        lambda c, b=b: _fm_rows(h2_sc, c, b), "gelu", ind)
        h2b = h2_sc.rearrange("(c p) T -> p c T", p=128)
        ind = _load_inb(k, inb, [
            h2b[:, q * 8:(q + 1) * 8, b * TB:(b + 1) * TB] for q in range(4)])
        _fm_chain_layer(k, ar, bufs, "w_fh2", "s_fh2_col", 32, list(range(32)),
                        lambda c, b=b: _fm_rows(h3_sc, c, b), "gelu", ind)
        h3b = h3_sc.rearrange("(c p) T -> p c T", p=128)
        ind = _load_inb(k, inb, [
            h3b[:, q * 8:(q + 1) * 8, b * TB:(b + 1) * TB] for q in range(4)])
        _fm_chain_layer(k, ar, bufs, "w_f3", "s_f3_col", 32, list(range(8)),
                        lambda c, b=b: _fm_rows(y_sc, c, b), "copy", ind)
        ar.close()


def ph_moe(k, xnT, y_sc32):
    p = k.p
    onr = k.c["ones_row_bf"]
    idb = k.c["ident_bf"]
    for b in range(NB):
        ar = Arena(k.nc, p)
        inb = ar.sb("mo_in", [128, 8 * TB], BF16)
        yacc = ar.sb("mo_y", [128, 8 * TB])
        m_fm = ar.sb("mo_mfm", [32, TB], BF16)
        h_sb = ar.sb("mo_h", [128, 8 * 512], BF16)
        sil = ar.sb("mo_sil", [128, 4 * 512], BF16)
        t1b = ar.sb("mo_t1", [128, 4 * 512], BF16)
        mbc = ar.sb("mo_mbc", [128, 512])
        wg_sb = ar.sb("mo_wg", [128, 8 * 32], BF16)
        bank_free = [None] * 8

        in_done = _load_inb(k, inb, [k.t_xnT_slab(xnT, b, 0, TB)])
        p.dma("gpsimd", wg_sb[:], k.t["wg"].rearrange("(kc p) e -> p kc e", p=128))

        gar = Arena(k.nc, p)
        psg = [gar.ps("mo_psg", [128, 32]) for _ in range(2)]
        pst = [gar.ps("mo_pst", [32, 128], BF16) for _ in range(2)]
        gst = [gar.sb("mo_gst", [128, 8]) for _ in range(2)]
        ex = [gar.sb("mo_ex", [128, 32]) for _ in range(2)]
        top8 = [gar.sb("mo_top8", [128, 8]) for _ in range(2)]
        mtm = [gar.sb("mo_mtm", [128, 32], BF16) for _ in range(2)]
        g_free = [None, None]
        t_free = [None, None]
        ex_rd = [0, 0]
        mtm_rd = [0, 0]
        p.barrier()
        for tt in range(TB // 128):
            r = tt % 2
            w = []
            if g_free[r] is not None:
                w.append(g_free[r])
            for kc in range(8):
                p.op("tensor", lambda e, kc=kc, r=r, tt=tt: e.matmul(
                    out=psg[r][:], lhsT=inb[:, kc * TB + tt * 128: kc * TB + tt * 128 + 128],
                    rhs=wg_sb[:, kc * 32:(kc + 1) * 32],
                    start=(kc == 0), stop=(kc == 7)), wait=(w if kc == 0 else []))
            mm_g = p.n("tensor")
            v1 = p.op("vector", lambda e, r=r: e.reduce_max(
                out=gst[r][:, 0:1], in_=psg[r][:], axis=AX.X), wait=[("tensor", mm_g)])
            vneg = p.op("vector", lambda e, r=r: e.tensor_scalar_mul(
                gst[r][:, 1:2], gst[r][:, 0:1], -1.0))
            a1 = p.op("scalar", lambda e, r=r: e.activation(
                out=ex[r][:], in_=psg[r][:], func=AF.Exp, bias=gst[r][:, 1:2],
                accum_out=gst[r][:, 2:3]),
                wait=[("vector", vneg), ("vector", ex_rd[r])])
            g_free[r] = ("scalar", a1)
            p.op("vector", lambda e, r=r: e.reciprocal(
                gst[r][:, 3:4], gst[r][:, 2:3]), wait=[("scalar", a1)])
            p.op("vector", lambda e, r=r: e.max(out=top8[r][:], in_=ex[r][:]))
            p.op("vector", lambda e, r=r: e.tensor_scalar(
                out=mtm[r][:], in0=ex[r][:], scalar1=top8[r][:, 3:4],
                scalar2=gst[r][:, 3:4], op0=ALU.is_ge, op1=ALU.mult),
                wait=[("tensor", mtm_rd[r])])
            vm = p.op("vector", lambda e, r=r: e.tensor_mul(
                mtm[r][:], mtm[r][:], ex[r][:]))
            ex_rd[r] = vm
            w = [("vector", vm)]
            if t_free[r] is not None:
                w.append(t_free[r])
            mm_t = p.op("tensor", lambda e, r=r: e.transpose(
                out=pst[r][:], in_=mtm[r][:], identity=idb[:]), wait=w)
            mtm_rd[r] = mm_t
            ev = p.op("scalar", lambda e, r=r, tt=tt: e.copy(
                out=m_fm[:, tt * 128:(tt + 1) * 128], in_=pst[r][:]),
                wait=[("tensor", mm_t)])
            t_free[r] = ("scalar", ev)
        p.barrier()
        gar.close()

        ps = [ar.ps("mo_ps", [128, 512]) for _ in range(8)]

        sar = Arena(k.nc, p)
        sil_free = [0, 0, 0, 0]
        wsha = sar.sb("mo_wsha", [128, 3 * 8192], BF16)
        for wi, wn in enumerate(["w_s1", "w_s3", "w_s2"]):
            p.dma("gpsimd", wsha[:, wi * 8192:(wi + 1) * 8192].rearrange(
                "p (oc x) -> p oc x", oc=8),
                k.t[wn].rearrange("oc p x -> p oc x"))
        p.barrier()
        for tc in range(4):
            rhs_off = tc * 512
            for r2 in range(2):
                for br in range(8):
                    hg = r2 * 4 + (br % 4)
                    woff = (0 if br < 4 else 8192) + hg * 1024
                    bank = br if br < 4 else br
                    w = [bank_free[br]] if bank_free[br] else []
                    for kc in range(8):
                        p.op("tensor", lambda e, br=br, kc=kc, woff=woff, rhs_off=rhs_off:
                             e.matmul(out=ps[br][:],
                                      lhsT=wsha[:, woff + kc * 128: woff + kc * 128 + 128],
                                      rhs=inb[:, kc * TB + rhs_off: kc * TB + rhs_off + 512],
                                      start=(kc == 0), stop=(kc == 7)),
                             wait=(w if kc == 0 else []))
                mm_end = p.n("tensor")
                for hc in range(4):
                    hg = r2 * 4 + hc
                    a = p.op("scalar", lambda e, hc=hc: e.activation(
                        out=sil[:, hc * 512:(hc + 1) * 512], in_=ps[hc][:],
                        func=AF.Silu),
                        wait=[("tensor", mm_end - (7 - hc) * 8),
                              ("vector", sil_free[hc])])
                    bank_free[hc] = ("scalar", a)
                    v = p.op("vector", lambda e, hc=hc, hg=hg: e.tensor_mul(
                        h_sb[:, hg * 512:(hg + 1) * 512],
                        sil[:, hc * 512:(hc + 1) * 512], ps[4 + hc][:]),
                        wait=[("scalar", a), ("tensor", mm_end - (3 - hc) * 8)])
                    sil_free[hc] = v
                    bank_free[4 + hc] = ("vector", v)
            h_done = p.n("vector")
            for oc in range(8):
                rb = oc % 2
                w = [bank_free[rb]] if bank_free[rb] else []
                w.append(("vector", h_done))
                for kc in range(8):
                    p.op("tensor", lambda e, oc=oc, kc=kc, rb=rb, rhs_off=rhs_off:
                         e.matmul(out=ps[rb][:],
                                  lhsT=wsha[:, 16384 + oc * 1024 + kc * 128: 16384 + oc * 1024 + kc * 128 + 128],
                                  rhs=h_sb[:, kc * 512:(kc + 1) * 512],
                                  start=(kc == 0), stop=(kc == 7)),
                         wait=(w if kc == 0 else []))
                a = p.op("scalar", lambda e, oc=oc, rb=rb, rhs_off=rhs_off: e.copy(
                    out=yacc[:, oc * TB + rhs_off: oc * TB + rhs_off + 512],
                    in_=ps[rb][:]), wait=[("tensor", p.n("tensor"))])
                bank_free[rb] = ("scalar", a)
        p.barrier()
        sar.close()

        ear = Arena(k.nc, p)
        wex = [ear.sb("mo_wex", [128, 12288], BF16) for _ in range(2)]
        mrow = [ear.sb("mo_mrow", [1, TB], BF16) for _ in range(2)]
        wex_rd = [0, 0]
        mrow_rd = [0, 0]
        mbc_rd = 0
        sil_rd = 0
        h_rd = 0
        t1_free = [0, 0, 0, 0]
        for e_i in range(32):
            re = e_i % 2
            w = [("tensor", wex_rd[re])] if wex_rd[re] else []
            wd = p.dma("gpsimd", wex[re][:, 0:4096], k.t["w_e1"][e_i], wait=w)
            wd = p.dma("gpsimd", wex[re][:, 4096:8192], k.t["w_e3"][e_i])
            wd = p.dma("gpsimd", wex[re][:, 8192:12288], k.t["w_e2"][e_i])
            w = [("tensor", mrow_rd[re])] if mrow_rd[re] else []
            md = p.dma("sync", mrow[re][0:1, :], m_fm[e_i:e_i + 1, :], wait=w)
            for tc in range(4):
                w = [("sync", md)]
                if bank_free[0]:
                    w.append(bank_free[0])
                mm_m = p.op("tensor", lambda e, re=re, tc=tc: e.matmul(
                    out=ps[0][:], lhsT=onr[0:1, :],
                    rhs=mrow[re][0:1, tc * 512:(tc + 1) * 512],
                    start=True, stop=True), wait=w)
                if tc == 3:
                    mrow_rd[re] = mm_m
                w = [("tensor", mm_m)]
                if mbc_rd:
                    w.append(("gpsimd", mbc_rd))
                ev_m = p.op("scalar", lambda e: e.copy(out=mbc[:], in_=ps[0][:]),
                            wait=w)
                bank_free[0] = ("scalar", ev_m)
                for br in range(8):
                    woff = (0 if br < 4 else 4096) + (br % 4) * 1024
                    w = [bank_free[br]] if bank_free[br] else []
                    w.append(("gpsimd", wd))
                    for kc in range(8):
                        mm = p.op("tensor", lambda e, br=br, kc=kc, woff=woff,
                                  re=re, tc=tc:
                             e.matmul(out=ps[br][:],
                                      lhsT=wex[re][:, woff + kc * 128: woff + kc * 128 + 128],
                                      rhs=inb[:, kc * TB + tc * 512: kc * TB + tc * 512 + 512],
                                      start=(kc == 0), stop=(kc == 7)),
                             wait=(w if kc == 0 else []))
                mm_end = p.n("tensor")
                if tc == 3:
                    wex_rd[re] = mm_end
                for hc in range(4):
                    a = p.op("scalar", lambda e, hc=hc: e.activation(
                        out=sil[:, hc * 512:(hc + 1) * 512], in_=ps[hc][:],
                        func=AF.Silu),
                        wait=[("tensor", mm_end - (7 - hc) * 8),
                              ("gpsimd", sil_rd)])
                    bank_free[hc] = ("scalar", a)
                    g = p.op("gpsimd", lambda e, hc=hc: e.tensor_mul(
                        t1b[:, hc * 512:(hc + 1) * 512],
                        sil[:, hc * 512:(hc + 1) * 512], mbc[:]),
                        wait=[("scalar", a), ("scalar", ev_m),
                              ("vector", t1_free[hc])])
                    v = p.op("vector", lambda e, hc=hc: e.tensor_mul(
                        h_sb[:, hc * 512:(hc + 1) * 512],
                        t1b[:, hc * 512:(hc + 1) * 512], ps[4 + hc][:]),
                        wait=[("gpsimd", g), ("tensor", mm_end - (3 - hc) * 8),
                              ("tensor", h_rd)])
                    t1_free[hc] = v
                    bank_free[4 + hc] = ("vector", v)
                sil_rd = p.n("gpsimd")
                mbc_rd = p.n("gpsimd")
                h_done = [0] * 4
                h_done = {hc: None for hc in range(4)}
                hv = p.n("vector")
                for oc in range(8):
                    rb = 2 + oc % 2
                    w = [bank_free[rb]] if bank_free[rb] else []
                    w.append(("vector", hv))
                    for kc in range(4):
                        mm = p.op("tensor", lambda e, oc=oc, kc=kc, rb=rb, re=re:
                             e.matmul(out=ps[rb][:],
                                      lhsT=wex[re][:, 8192 + (oc * 4 + kc) * 128: 8192 + (oc * 4 + kc) * 128 + 128],
                                      rhs=h_sb[:, kc * 512:(kc + 1) * 512],
                                      start=(kc == 0), stop=(kc == 3)),
                             wait=(w if kc == 0 else []))
                    if tc == 3 and oc >= 6:
                        wex_rd[re] = max(wex_rd[re], p.n("tensor"))
                    if oc == 7:
                        h_rd = p.n("tensor")
                    v = p.op("vector", lambda e, oc=oc, rb=rb, tc=tc: e.tensor_add(
                        yacc[:, oc * TB + tc * 512: oc * TB + tc * 512 + 512],
                        ps[rb][:],
                        yacc[:, oc * TB + tc * 512: oc * TB + tc * 512 + 512]),
                        wait=[("tensor", p.n("tensor"))])
                    bank_free[rb] = ("vector", v)
        yv = p.n("vector")
        for oc in range(8):
            p.dma("sync", _fm_rows(y_sc32, oc, b),
                  yacc[:, oc * TB:(oc + 1) * TB], wait=[("vector", yv)])
        p.barrier()
        ear.close()
        ar.close()


MOE_W = ["wg", "w_e1", "w_e3", "w_e2", "w_s1", "w_s3", "w_s2"]


def build_moe_launch(host, dbg=False):
    nc = bass.Bass("TRN2", target_bir_lowering=False, debug=False)
    k = K(nc, host)
    kd = "ExternalOutput" if dbg else "Internal"
    x_in = k.dram("x_in", [TOK, D], F32, kind="ExternalInput")
    x_out = k.dram("x_out", [TOK, D], F32, kind="ExternalOutput")
    k.declare_weights(MOE_W)
    xnT = [k.dram(f"xnT{b}", [D, TB + 1], BF16, kind=kd) for b in range(NB)]
    y_sc32 = k.dram("y_sc32", [D, TOK], F32, kind=kd)
    ph_consts(k, ["ident_bf", "ident_f32", "ones_row_bf"])
    ph_preln(k, x_in, xnT)
    ph_moe(k, xnT, y_sc32)
    ph_post(k, y_sc32, True, x_in, x_out)
    k.p.finalize()
    return k


L0FFN_W = ["w_f12", "s_f12_col", "w_fh1", "s_fh1_col", "w_fh2", "s_fh2_col",
           "w_f3", "s_f3_col"]
SMALL_SB = {
    "ident_bf": False, "ident_f32": False, "ones_row_bf": False,
    "cmask": False, "cmaskT": False, "E5": False, "E5T": False,
    "pfx_bf": False, "wdn_col": False, "gcol_bf": False, "tloc_col": False,
    "s_f12_col": False, "s_fh1_col": False, "s_fh2_col": False,
    "s_f3_col": False,
    "b_qkv_col": False, "b_memv_col": False, "b_l0out_col": False,
    "b_memg_tile": False, "memscale_tile": False,
    "b_qkv_v_row": True,
}


def build_l0ffn_launch(host, dbg=False):
    nc = bass.Bass("TRN2", target_bir_lowering=False, debug=False)
    k = K(nc, host)
    kd = "ExternalOutput" if dbg else "Internal"
    x_in = k.dram("x_in", [TOK, D], F32, kind="ExternalInput")
    x_out = k.dram("x_out", [TOK, D], F32, kind="ExternalOutput")
    k.declare_weights(L0FFN_W)
    xnT = [k.dram(f"xnT{b}", [D, TB + 1], BF16, kind=kd) for b in range(NB)]
    h_sc = k.dram("h_sc", [4096, TOK], BF16, kind=kd)
    h2_sc = k.dram("h2_sc", [4096, TOK], BF16, kind=kd)
    h3_sc = k.dram("h3_sc", [4096, TOK], BF16, kind=kd)
    y_sc = k.dram("y_sc", [D, TOK], BF16, kind=kd)
    ph_consts(k, ["ident_bf", "s_f12_col", "s_fh1_col", "s_fh2_col", "s_f3_col"])
    ph_preln(k, x_in, xnT)
    ph_l0ffn(k, xnT, h_sc, h2_sc, h3_sc, y_sc)
    ph_post(k, y_sc, False, x_in, x_out)
    k.p.finalize()
    return k, L0FFN_W


def _shard_x(inputs):
    x = np.asarray(inputs["L0_attn_x"], np.float32)
    return [np.ascontiguousarray(x[2 * c:2 * c + 2].reshape(TOK, D))
            for c in range(NCORES)]


SLOPES = [float(2.0 ** (-(h + 1) * 0.5)) for h in range(16)]
DBG_ATTN = [None]
DBG_ATTN2 = [None]
DBG_MEM = [None]
DBG_GROW = [None]
DBG_GBC = [None]
DBG_SUMS = [None]
DBG_SUMS2 = [None]
DBG_MVB = [None]
DBG_MEM2 = [None]


def ph_attn_core(k, blk, b, ar, inb, qsb, ksb, vsb, seq_sc):
    p = k.p
    onr = k.c["ones_row_bf"]
    win = (blk == 1)
    aar = Arena(k.nc, p)
    kaug = aar.sb("at_kaug", [65, TB], BF16)
    qaug = aar.sb("at_qaug", [65, TB], BF16)
    nmz = aar.sb("at_nmz", [128, 32])
    nb_bf = aar.sb("at_nbbf", [128, 2], BF16)
    zp = aar.sb("at_zp", [128, 8])
    rows32 = aar.sb("at_rows", [32, 128], BF16)
    rzrow = aar.sb("at_rzrow", [1, TB], BF16)
    scr = aar.sb("at_scr", [128, 640])
    scr2 = [aar.sb("at_scr2", [128, 512]) for _ in range(2)]
    strip = aar.sb("at_strip", [128, 640])
    pstr = [aar.sb("at_pstr", [128, 512], BF16) for _ in range(16)]
    seqh = aar.sb("at_seqh", [64, TB], BF16)
    rb_sb = aar.sb("at_rb", [128, 512])
    psA = [aar.ps("at_psA", [128, 512]) for _ in range(4)]
    psB = [aar.ps("at_psB", [128, 512]) for _ in range(2)]
    psO = [aar.ps("at_psO", [64, 512]) for _ in range(1)]
    psR = aar.ps("at_psR", [128, 512])
    bfree = {}

    def bwait(key):
        return [bfree[key]] if key in bfree else []

    vo = p.op("vector", lambda e: e.memset(kaug[64:65, :], 1.0))
    kaug_rd = 0
    qaug_rd = 0
    seqh_dma = 0
    rows_rd = 0
    NQ = TB // 128
    for h in range(H):
        hc, hp = h // 2, (h % 2) * 64
        w = [("tensor", kaug_rd)] if kaug_rd else []
        vk = p.op("vector", lambda e, hc=hc, hp=hp: e.tensor_copy(
            kaug[0:64, :], ksb[hp:hp + 64, hc * TB:(hc + 1) * TB]), wait=w)
        w = [("tensor", qaug_rd)] if qaug_rd else []
        vq = p.op("vector", lambda e, hc=hc, hp=hp: e.tensor_copy(
            qaug[0:64, :], qsb[hp:hp + 64, hc * TB:(hc + 1) * TB]), wait=w)
        for qi in range(NQ):
            first = max(0, qi - 4) if win else 0
            chunks = []
            c0 = first * 128
            total = (qi + 1) * 128 - c0
            while total > 0:
                wd = min(512, total)
                chunks.append((c0, wd))
                c0 += wd
                total -= wd
            ps_set = psA
            qs = qaug[0:64, qi * 128:(qi + 1) * 128]
            mm_end = []
            for ci, (c0, wd) in enumerate(chunks):
                w = [("vector", max(vk, vq))] + bwait(("A", ci))
                mm = p.op("tensor", lambda e, ci=ci, c0=c0, wd=wd, qs=qs:
                          e.matmul(out=ps_set[ci][:, 0:wd], lhsT=qs,
                                   rhs=kaug[0:64, c0:c0 + wd],
                                   start=True, stop=True), wait=w)
                mm_end.append(mm)
            if win:
                sl = SLOPES[h]
                wslice = sum(wd for _, wd in chunks)
                off = 640 - wslice
                stts = []
                for ci, (c0, wd) in enumerate(chunks):
                    pos = off + (c0 - chunks[0][0])
                    v = p.op("vector", lambda e, ci=ci, pos=pos, wd=wd, sl=sl:
                             e.scalar_tensor_tensor(
                                 out=strip[:, pos:pos + wd],
                                 in0=k.c["E5"][:, pos:pos + wd], scalar=sl,
                                 in1=ps_set[ci][:, 0:wd],
                                 op0=ALU.mult, op1=ALU.add),
                             wait=[("tensor", mm_end[ci])] + bwait(("strip", 0)))
                    stts.append(v)
                    bfree[("A", ci)] = ("vector", v)
                vmx = p.op("vector", lambda e, off=off, wslice=wslice: e.reduce_max(
                    out=zp[:, 4:5], in_=strip[:, off:off + wslice], axis=AX.X))
            else:
                lc0, lwd = chunks[-1]
                vmsk = p.op("vector", lambda e, ci=len(chunks) - 1, lwd=lwd:
                            e.tensor_tensor(
                                out=ps_set[ci][:, lwd - 128:lwd],
                                in0=ps_set[ci][:, lwd - 128:lwd],
                                in1=k.c["cmask"][:], op=ALU.add),
                            wait=[("tensor", mm_end[-1])])
                mx = []
                for ci, (c0, wd) in enumerate(chunks):
                    v = p.op("vector", lambda e, ci=ci, wd=wd: e.reduce_max(
                        out=zp[:, ci:ci + 1], in_=ps_set[ci][:, 0:wd],
                        axis=AX.X),
                        wait=[("tensor", mm_end[ci]), ("vector", vmsk)])
                    mx.append(v)
                vmx = p.op("vector", lambda e, nc_=len(chunks): e.reduce_max(
                    out=zp[:, 4:5], in_=zp[:, 0:nc_], axis=AX.X))
            p.op("vector", lambda e: e.tensor_scalar_mul(
                zp[:, 5:6], zp[:, 4:5], -1.0))
            p.op("vector", lambda e: e.tensor_copy(nb_bf[:, 0:1], zp[:, 5:6]))
            vnm = p.op("vector", lambda e, qi=qi: e.tensor_copy(
                nmz[:, qi:qi + 1], nb_bf[:, 0:1]))
            if win:
                a = p.op("scalar", lambda e, off=off, wslice=wslice, qi=qi:
                         e.activation(out=scr[:, 0:wslice],
                                      in_=strip[:, off:off + wslice],
                                      func=AF.Exp, bias=nmz[:, qi:qi + 1],
                                      accum_out=zp[:, 0:1]),
                         wait=[("vector", vnm)])
                bfree[("strip", 0)] = ("scalar", a)
                nzp = 1
                if DBG_ATTN2[0] is not None and qi == 14:
                    p.dma("sync", DBG_ATTN2[0][b, h, 0], strip[:],
                          wait=[("vector", p.n("vector"))])
                    p.dma("sync", DBG_ATTN2[0][b, h, 1], scr[:, 0:640],
                          wait=[("scalar", a)])
            else:
                for ci, (c0, wd) in enumerate(chunks):
                    a = p.op("scalar", lambda e, ci=ci, wd=wd, qi=qi:
                             e.activation(out=scr[:, 0:wd],
                                          in_=ps_set[ci][:, 0:wd],
                                          func=AF.Exp, bias=nmz[:, qi:qi + 1],
                                          accum_out=zp[:, ci:ci + 1]),
                             wait=[("vector", vnm)])
                    bfree[("A", ci)] = ("scalar", a)
                nzp = len(chunks)
            vz = p.op("vector", lambda e, nzp=nzp: e.reduce_sum(
                out=zp[:, 6:7], in_=zp[:, 0:nzp], axis=AX.X),
                wait=[("scalar", a)])
            p.op("vector", lambda e, qi=qi: e.reciprocal(
                nmz[:, 16 + qi:17 + qi], zp[:, 6:7]))
        if DBG_ATTN[0] is not None:
            p.dma("sync", DBG_ATTN[0][b, h], nmz[:],
                  wait=[("vector", p.n("vector"))])
        w = [("tensor", rows_rd)] if rows_rd else []
        w += bwait(("R", 0))
        mm_t = p.op("tensor", lambda e: e.transpose(
            out=psR[0:32, 0:128], in_=nmz[:], identity=k.c["ident_f32"][:]),
            wait=[("vector", p.n("vector"))] + w)
        ev_t = p.op("scalar", lambda e: e.copy(out=rows32[:], in_=psR[0:32, 0:128]),
                    wait=[("tensor", mm_t)])
        bfree[("R", 0)] = ("scalar", ev_t)
        rows_rd = mm_t
        d1 = p.dma("sync", qaug[64:65, :], rows32[0:16, :],
                   wait=[("scalar", ev_t), ("tensor", qaug_rd)])
        d2 = p.dma("sync", rzrow[0:1, :], rows32[16:32, :],
                   wait=[("scalar", ev_t)])
        for g in range(NQ // 4):
            ts_lo = max(0, 4 * g - 4) if win else 0
            ts_hi = 4 * g + 3
            pv_i = 0
            exp_done = {}
            for ts in range(ts_lo, ts_hi + 1):
                m0 = max(ts, 4 * g)
                m1 = min(ts + 4, ts_hi) if win else ts_hi
                col0 = m0 * 128
                N = (m1 + 1) * 128 - col0
                pi = ts % 2
                w = [("sync", d1), ("vector", max(vk, vq))] + bwait(("B", pi))
                mm2 = p.op("tensor", lambda e, ts=ts, col0=col0, N=N, pi=pi:
                           e.matmul(out=psB[pi][:, 0:N],
                                    lhsT=kaug[:, ts * 128:(ts + 1) * 128],
                                    rhs=qaug[:, col0:col0 + N],
                                    start=True, stop=True), wait=w)
                lo = (m0 - 4 * g) * 128
                hi = lo + N
                wz = bwait(("pstr", ts))
                if lo > 0:
                    p.op("vector", lambda e, ts=ts, lo=lo: e.memset(
                        pstr[ts][:, 0:lo], 0.0), wait=wz)
                if hi < 512:
                    p.op("vector", lambda e, ts=ts, hi=hi: e.memset(
                        pstr[ts][:, hi:512], 0.0), wait=wz)
                if win:
                    sl = SLOPES[h]
                    doff = (m0 - ts) * 128
                    v2 = p.op("vector", lambda e, pi=pi, N=N, doff=doff, sl=sl:
                              e.scalar_tensor_tensor(
                                  out=scr2[pi][:, 0:N],
                                  in0=k.c["E5T"][:, doff:doff + N], scalar=sl,
                                  in1=psB[pi][:, 0:N],
                                  op0=ALU.mult, op1=ALU.add),
                              wait=[("tensor", mm2)] + bwait(("scr2", pi)))
                    bfree[("B", pi)] = ("vector", v2)
                    a2 = p.op("scalar", lambda e, ts=ts, lo=lo, N=N, pi=pi:
                              e.activation(out=pstr[ts][:, lo:lo + N],
                                           in_=scr2[pi][:, 0:N], func=AF.Exp),
                              wait=[("vector", v2), ("vector", p.n("vector"))])
                    bfree[("scr2", pi)] = ("scalar", a2)
                else:
                    wd2 = [("tensor", mm2)]
                    if ts >= 4 * g:
                        vm2 = p.op("vector", lambda e, pi=pi: e.tensor_tensor(
                            out=psB[pi][:, 0:128], in0=psB[pi][:, 0:128],
                            in1=k.c["cmaskT"][:], op=ALU.add),
                            wait=[("tensor", mm2)])
                        wd2.append(("vector", vm2))
                    a2 = p.op("scalar", lambda e, ts=ts, lo=lo, N=N, pi=pi:
                              e.activation(out=pstr[ts][:, lo:lo + N],
                                           in_=psB[pi][:, 0:N], func=AF.Exp),
                              wait=wd2 + [("vector", p.n("vector"))])
                    bfree[("B", pi)] = ("scalar", a2)
                exp_done[ts] = a2
            nts = ts_hi - ts_lo + 1
            for j, ts in enumerate(range(ts_lo, ts_hi + 1)):
                w = [("scalar", exp_done[ts]), ("vector", p.n("vector"))]
                if j == 0:
                    w += bwait(("O", 0))
                pv = p.op("tensor", lambda e, ts=ts, j=j, nts=nts, h=h:
                          e.matmul(out=psO[0][:],
                                   lhsT=vsb[:, ts * 1024 + h * 64: ts * 1024 + h * 64 + 64],
                                   rhs=pstr[ts][:, 0:512],
                                   start=(j == 0), stop=(j == nts - 1)), wait=w)
                bfree[("pstr", ts)] = ("tensor", pv)
            w = [("sync", d2)] + bwait(("R", 0))
            mmr = p.op("tensor", lambda e, g=g: e.matmul(
                out=psR[:], lhsT=onr[0:1, :], rhs=rzrow[0:1, g * 512:(g + 1) * 512],
                start=True, stop=True), wait=w)
            evr = p.op("scalar", lambda e: e.copy(out=rb_sb[:], in_=psR[:]),
                       wait=[("tensor", mmr)])
            bfree[("R", 0)] = ("scalar", evr)
            w = [("scalar", evr), ("tensor", pv)]
            if seqh_dma and g == 0:
                w.append(("sync", seqh_dma))
            vdiv = p.op("vector", lambda e, g=g: e.tensor_mul(
                seqh[:, g * 512:(g + 1) * 512], psO[0][:], rb_sb[0:64, :]),
                wait=w)
            bfree[("O", 0)] = ("vector", vdiv)
        kaug_rd = p.n("tensor")
        qaug_rd = p.n("tensor")
        seqh_dma = p.dma("sync", seq_sc.rearrange(
            "(c p) T -> p c T", p=64)[:, h, b * TB:(b + 1) * TB],
            seqh[:], wait=[("vector", p.n("vector"))])
    p.barrier()
    aar.close()


def _attn_proj(k, b, ar, inb, qsb, ksb, vsb, wname, wv_sb, bv_row, bq_col,
               bk_col):
    p = k.p
    onr = k.c["ones_row_bf"]
    par_ = Arena(k.nc, p)
    wt = [par_.sb("ap_w", [128, 8 * 128], BF16) for _ in range(2)]
    ps = [par_.ps("ap_ps", [128, 512]) for _ in range(8)]
    wdram = k.t[wname]
    st = {"w_rd": [0, 0], "ps_free": [None, None]}
    first = True
    for step, oc in enumerate(range(16)):
        r = step % 2
        par = r * 4
        w = [("tensor", st["w_rd"][r])] if st["w_rd"][r] else []
        wdma = p.dma("gpsimd", wt[r][:], wdram[oc], wait=w)
        mw = [("gpsimd", wdma)]
        if first:
            mw += [("sync", p.n("sync"))]
            first = False
        if st["ps_free"][r] is not None:
            mw.append(st["ps_free"][r])
        for kc in range(8):
            for t4 in range(4):
                p.op("tensor", lambda e, kc=kc, t4=t4, r=r, par=par:
                     e.matmul(out=ps[par + t4][:],
                              lhsT=wt[r][:, kc * 128:(kc + 1) * 128],
                              rhs=inb[:, kc * (TB + 1) + 1 + t4 * 512:
                                      kc * (TB + 1) + 1 + t4 * 512 + 512],
                              start=(kc == 0), stop=(kc == 7)),
                     wait=(mw if kc == 0 and t4 == 0 else []))
        mm_end = p.n("tensor")
        st["w_rd"][r] = mm_end
        dst = qsb if oc < 8 else ksb
        c = oc % 8
        evacs = []
        for t4 in range(4):
            mmw = [("tensor", mm_end - (3 - t4))]
            kw = {}
            if oc < 8:
                kw["scale"] = 0.125
                if bq_col is not None:
                    kw["bias"] = bq_col[:, c:c + 1]
            else:
                if bk_col is not None:
                    kw["bias"] = bk_col[:, c:c + 1]
            fnq = AF.Identity if "bias" in kw else AF.Copy
            evacs.append(p.op("scalar", lambda e, t4=t4, par=par, dst=dst,
                              c=c, kw=kw, fnq=fnq:
                e.activation(out=dst[:, c * TB + t4 * 512: c * TB + t4 * 512 + 512],
                             in_=ps[par + t4][:], func=fnq, **kw),
                wait=mmw))
        st["ps_free"][r] = ("scalar", evacs[-1])
    p.barrier()
    par_.close()
    par_ = Arena(k.nc, p)
    vps = [par_.ps("ap_vps", [128, 512]) for _ in range(4)]
    v_free = [None, None]
    for tt in range(TB // 128):
        r = tt % 2
        w = []
        if v_free[r] is not None:
            w.append(v_free[r])
        nb = 0 if bv_row is None else 1
        for kc in range(8):
            for half in range(2):
                p.op("tensor", lambda e, kc=kc, half=half, r=r, tt=tt:
                     e.matmul(out=vps[r * 2 + half][:],
                              lhsT=inb[:, kc * (TB + 1) + 1 + tt * 128:
                                       kc * (TB + 1) + 1 + tt * 128 + 128],
                              rhs=wv_sb[:, kc * 1024 + half * 512:
                                        kc * 1024 + half * 512 + 512],
                              start=(kc == 0), stop=(kc == 7 and nb == 0)),
                     wait=(w if kc == 0 and half == 0 else []))
        if nb:
            for half in range(2):
                p.op("tensor", lambda e, half=half, r=r: e.matmul(
                    out=vps[r * 2 + half][:], lhsT=onr[0:1, :],
                    rhs=bv_row[0:1, half * 512: half * 512 + 512],
                    start=False, stop=True))
        mm_end = p.n("tensor")
        evacs = []
        for half in range(2):
            evacs.append(p.op("scalar", lambda e, half=half, r=r, tt=tt:
                e.copy(out=vsb[:, tt * 1024 + half * 512: tt * 1024 + half * 512 + 512],
                       in_=vps[r * 2 + half][:]),
                wait=[("tensor", mm_end - (1 - half))]))
        v_free[r] = ("scalar", evacs[-1])
    p.barrier()
    par_.close()


def _mem_path(k, b, inb, wproj_sb, grow_row, in_done=()):
    p = k.p
    onr = k.c["ones_row_bf"]
    ar = Arena(k.nc, p)
    w1s = ar.sb("mp_w1s", [128, 64])
    r1s = ar.sb("mp_r1s", [128, 64])
    pair = ar.sb("mp_pair", [128, 256])
    wl = ar.sb("mp_wl", [128, 96])
    rls = ar.sb("mp_rls", [128, 96])
    jju = ar.sb("mp_jju", [128, 576])
    jjs_bf = ar.sb("mp_jjs", [128, 576], BF16)
    jjw_bf = ar.sb("mp_jjw", [128, 576], BF16)
    rru = ar.sb("mp_rru", [128, 576])
    dt5 = ar.sb("mp_dt5", [128, 576])
    s16 = ar.sb("mp_s16", [128, 8 * 4])
    m16 = ar.sb("mp_m16", [128, 16])
    g16 = ar.sb("mp_g16", [128, 16])
    Cf = ar.sb("mp_C", [1, 576])
    Cbf = ar.sb("mp_Cbf", [1, 576], BF16)
    gcols = ar.sb("mp_gcols", [128, 16])
    grsb = ar.sb("mp_grsb", [16, 128], BF16)
    pw1 = ar.ps("mp_pw1", [128, 64])
    pwrg = ar.ps("mp_pwrg", [128, 208])
    psP = ar.ps("mp_psP", [128, 512])
    psP2 = ar.ps("mp_psP2", [128, 64])
    psG = ar.ps("mp_psG", [1, 512])
    psG2 = ar.ps("mp_psG2", [1, 64])
    pt16 = ar.ps("mp_pt16", [16, 128])
    bfree = {}

    def bw(key):
        return [bfree[key]] if key in bfree else []

    vC = p.op("vector", lambda e: e.memset(Cf[:], 0.0))
    vCb = p.op("vector", lambda e: e.memset(Cbf[:], 0.0))
    cbf_done = vCb
    v6 = lambda t, i: t.rearrange("p (h k) -> p h k", k=6)[:, :, i:i + 1]
    p16 = lambda i: pair.rearrange("p (h k) -> p h k", k=16)[:, :, i:i + 1]
    _mp_st = 9
    for tt in range(TB // 128):
        w = bw("pw1") + (list(in_done) if tt == 0 else [])
        for kc in range(8):
            p.op("tensor", lambda e, kc=kc, tt=tt: e.matmul(
                out=pw1[:], lhsT=inb[:, kc * (TB + 1) + tt * 128:
                                     kc * (TB + 1) + tt * 128 + 128],
                rhs=wproj_sb[:, kc * 272: kc * 272 + 64],
                start=(kc == 0), stop=(kc == 7)),
                wait=(w if kc == 0 else []))
        mm_w1 = p.n("tensor")
        w = bw("pwrg") + (list(in_done) if tt == 0 else [])
        for kc in range(8):
            p.op("tensor", lambda e, kc=kc, tt=tt: e.matmul(
                out=pwrg[:], lhsT=inb[:, kc * (TB + 1) + 1 + tt * 128:
                                      kc * (TB + 1) + 1 + tt * 128 + 128],
                rhs=wproj_sb[:, kc * 272 + 64: kc * 272 + 272],
                start=(kc == 0), stop=(kc == 7)),
                wait=(w if kc == 0 else []))
        mm_wrg = p.n("tensor")
        a_w1 = p.op("scalar", lambda e: e.copy(out=w1s[:], in_=pw1[:]),
                    wait=[("tensor", mm_w1)] + bw("w1s"))
        bfree["pw1"] = ("scalar", a_w1)
        vv = p.op("vector", lambda e: e.tensor_tensor(
            out=pair.rearrange("p (h i j) -> p h i j", i=4, j=4),
            in0=w1s.rearrange("p (h i) -> p h i", i=4).unsqueeze(3)
                .broadcast_to([128, 16, 4, 4]),
            in1=pwrg[:, 0:64].rearrange("p (h j) -> p h j", j=4).unsqueeze(2)
                .broadcast_to([128, 16, 4, 4]),
            op=ALU.mult), wait=[("scalar", a_w1), ("tensor", mm_wrg)])
        bfree["w1s"] = ("vector", vv)
        for kk, (i, j) in enumerate(PAIRS):
            p.op("vector", lambda e, kk=kk, i=i, j=j: e.tensor_sub(
                v6(wl, kk), p16(i * 4 + j), p16(j * 4 + i)))
        a_r1 = p.op("vector", lambda e: e.tensor_copy(r1s[:], pwrg[:, 64:128]),
                    wait=[("tensor", mm_wrg)] + bw("r1s"))
        vr = p.op("vector", lambda e: e.tensor_tensor(
            out=pair.rearrange("p (h i j) -> p h i j", i=4, j=4),
            in0=r1s.rearrange("p (h i) -> p h i", i=4).unsqueeze(3)
                .broadcast_to([128, 16, 4, 4]),
            in1=pwrg[:, 128:192].rearrange("p (h j) -> p h j", j=4).unsqueeze(2)
                .broadcast_to([128, 16, 4, 4]),
            op=ALU.mult))
        bfree["r1s"] = ("vector", vr)
        for kk in range(6):
            a2, b2 = PAIRS[5 - kk]
            if SIGN6[kk] > 0:
                p.op("vector", lambda e, kk=kk, a2=a2, b2=b2: e.tensor_sub(
                    v6(rls, kk), p16(a2 * 4 + b2), p16(b2 * 4 + a2)))
            else:
                p.op("vector", lambda e, kk=kk, a2=a2, b2=b2: e.tensor_sub(
                    v6(rls, kk), p16(b2 * 4 + a2), p16(a2 * 4 + b2)))
        if _mp_st < 2:
            continue
        vjj = p.op("vector", lambda e: e.tensor_tensor(
            out=jju.rearrange("p (h i j) -> p h i j", i=6, j=6),
            in0=wl.rearrange("p (h i) -> p h i", i=6).unsqueeze(3)
                .broadcast_to([128, 16, 6, 6]),
            in1=wl.rearrange("p (h j) -> p h j", j=6).unsqueeze(2)
                .broadcast_to([128, 16, 6, 6]),
            op=ALU.mult))
        p.op("vector", lambda e: e.reduce_sum(
            out=s16[:, 0:16],
            in_=bass_diag36(jju), axis=AX.X))
        p.op("vector", lambda e: e.tensor_scalar_max(
            s16[:, 0:16], s16[:, 0:16], 1e-24))
        viw = p.op("vector", lambda e: e.reciprocal(s16[:, 16:32],
                                                    s16[:, 0:16]))
        vjs = p.op("vector", lambda e: e.tensor_tensor(
            out=jjs_bf.rearrange("p (h x) -> p h x", x=36),
            in0=jju.rearrange("p (h x) -> p h x", x=36),
            in1=s16[:, 16:32].unsqueeze(2).broadcast_to([128, 16, 36]),
            op=ALU.mult), wait=bw("jjs"))
        vjw = p.op("vector", lambda e: e.tensor_scalar(
            out=jjw_bf[:], in0=jjs_bf[:], scalar1=k.c["wdn_col"][:, 0:1],
            scalar2=None, op0=ALU.mult), wait=bw("jjw"))
        vrr = p.op("vector", lambda e: e.tensor_tensor(
            out=rru.rearrange("p (h i j) -> p h i j", i=6, j=6),
            in0=rls.rearrange("p (h i) -> p h i", i=6).unsqueeze(3)
                .broadcast_to([128, 16, 6, 6]),
            in1=rls.rearrange("p (h j) -> p h j", j=6).unsqueeze(2)
                .broadcast_to([128, 16, 6, 6]),
            op=ALU.mult))
        p.op("vector", lambda e: e.reduce_sum(
            out=s16[:, 0:16], in_=bass_diag36(rru), axis=AX.X))
        p.op("vector", lambda e: e.tensor_scalar_max(
            s16[:, 0:16], s16[:, 0:16], 1e-24))
        vir = p.op("vector", lambda e: e.reciprocal(s16[:, 0:16],
                                                    s16[:, 0:16]))
        if _mp_st < 3:
            continue
        w = bw("psP") + [("vector", vjw)]
        mmP = p.op("tensor", lambda e: e.matmul(
            out=psP[:], lhsT=k.c["pfx_bf"][:], rhs=jjw_bf[:, 0:512],
            start=True, stop=False), wait=w + [("vector", cbf_done)])
        mmP = p.op("tensor", lambda e: e.matmul(
            out=psP[:], lhsT=onr[0:1, :], rhs=Cbf[0:1, 0:512],
            start=False, stop=True))
        w = bw("psP2")
        mmP2 = p.op("tensor", lambda e: e.matmul(
            out=psP2[:], lhsT=k.c["pfx_bf"][:], rhs=jjw_bf[:, 512:576],
            start=True, stop=False), wait=w)
        mmP2 = p.op("tensor", lambda e: e.matmul(
            out=psP2[:], lhsT=onr[0:1, :], rhs=Cbf[0:1, 512:576],
            start=False, stop=True))
        w = bw("psG") + [("vector", vjs)]
        mmG = p.op("tensor", lambda e: e.matmul(
            out=psG[:], lhsT=k.c["gcol_bf"][:, 0:1], rhs=jjs_bf[:, 0:512],
            start=True, stop=True), wait=w)
        w = bw("psG2")
        mmG2 = p.op("tensor", lambda e: e.matmul(
            out=psG2[:], lhsT=k.c["gcol_bf"][:, 0:1], rhs=jjs_bf[:, 512:576],
            start=True, stop=True), wait=w)
        vc1 = p.op("vector", lambda e: e.scalar_tensor_tensor(
            out=Cf[0:1, 0:512], in0=Cf[0:1, 0:512], scalar=float(DECAY ** 128),
            in1=psG[:], op0=ALU.mult, op1=ALU.add), wait=[("tensor", mmG)])
        bfree["psG"] = ("vector", vc1)
        vc2 = p.op("vector", lambda e: e.scalar_tensor_tensor(
            out=Cf[0:1, 512:576], in0=Cf[0:1, 512:576], scalar=float(DECAY ** 128),
            in1=psG2[:], op0=ALU.mult, op1=ALU.add), wait=[("tensor", mmG2)])
        bfree["psG2"] = ("vector", vc2)
        cbf_done = p.op("vector", lambda e: e.tensor_copy(Cbf[:], Cf[:]),
                        wait=[("tensor", mmP), ("tensor", mmP2)])
        if _mp_st < 4:
            bfree["psP"] = ("tensor", mmP)
            bfree["psP2"] = ("tensor", mmP2)
            bfree["jjw"] = ("tensor", mmP)
            bfree["jjs"] = ("tensor", mmG2)
            continue
        vd1 = p.op("vector", lambda e: e.tensor_mul(
            dt5[:, 0:512], rru[:, 0:512], psP[:]),
            wait=[("tensor", mmP), ("vector", vrr)])
        bfree["psP"] = ("vector", vd1)
        vd2 = p.op("vector", lambda e: e.tensor_mul(
            dt5[:, 512:576], rru[:, 512:576], psP2[:]), wait=[("tensor", mmP2)])
        bfree["psP2"] = ("vector", vd2)
        bfree["jjw"] = ("tensor", mmP)
        bfree["jjs"] = ("tensor", mmG2)
        p.op("vector", lambda e: e.reduce_sum(
            out=m16[:], in_=dt5.rearrange("p (h x) -> p h x", x=36), axis=AX.X))
        if DBG_MEM[0] is not None:
            p.barrier()
            p.dma("sync", DBG_MEM[0][tt, 0], m16[:], wait=[("vector", p.n("vector"))])
            p.dma("sync", DBG_MEM[0][tt, 1], s16[:, 0:16], wait=[("vector", p.n("vector"))])
            p.barrier()
        if DBG_MEM2[0] is not None:
            p.barrier()
            p.dma("sync", DBG_MEM2[0][tt, 0], rru[:], wait=[("vector", p.n("vector"))])
            p.dma("sync", DBG_MEM2[0][tt, 1], dt5[:], wait=[("vector", p.n("vector"))])
            p.barrier()
        p.op("vector", lambda e: e.tensor_scalar(
            out=s16[:, 0:16], in0=s16[:, 0:16],
            scalar1=k.c["tloc_col"][:, 0:1], scalar2=None, op0=ALU.mult),
            wait=[("vector", vir)])
        p.op("vector", lambda e: e.tensor_mul(m16[:], m16[:], s16[:, 0:16]))
        if DBG_MEM[0] is not None:
            p.barrier()
            p.dma("sync", DBG_MEM[0][tt, 2], m16[:], wait=[("vector", p.n("vector"))])
            p.barrier()
        vmm = p.op("vector", lambda e: e.tensor_mul(
            m16[:], m16[:], k.c["memscale_tile"][:]))
        a_sg = p.op("scalar", lambda e: e.activation(
            out=m16[:], in_=m16[:], func=AF.Sigmoid), wait=[("vector", vmm)])
        vg = p.op("vector", lambda e: e.scalar_tensor_tensor(
            out=g16[:], in0=pwrg[:, 192:208], scalar=1.0,
            in1=k.c["b_memg_tile"][:], op0=ALU.mult, op1=ALU.add),
            wait=[("tensor", mm_wrg)])
        bfree["pwrg"] = ("vector", vg)
        if DBG_MEM[0] is not None:
            p.dma("sync", DBG_MEM[0][tt, 3], g16[:], wait=[("vector", vg)])
        a_g = p.op("scalar", lambda e: e.activation(
            out=g16[:], in_=g16[:], func=AF.Sigmoid), wait=[("vector", vg)])
        p.op("vector", lambda e: e.tensor_mul(m16[:], m16[:], g16[:]),
             wait=[("scalar", a_sg), ("scalar", a_g)])
        p.op("vector", lambda e: e.reduce_sum(
            out=s16[:, 20:21], in_=m16[:], axis=AX.X))
        vgc = p.op("vector", lambda e, tt=tt: e.tensor_scalar_mul(
            gcols[:, tt:tt + 1], s16[:, 20:21], 1.0 / 16.0))
    mm_t = p.op("tensor", lambda e: e.transpose(
        out=pt16[:], in_=gcols[:], identity=k.c["ident_f32"][:]),
        wait=[("vector", p.n("vector"))])
    ev = p.op("scalar", lambda e: e.copy(out=grsb[:], in_=pt16[:]),
              wait=[("tensor", mm_t)])
    p.dma("sync", grow_row[0:1, :], grsb[:], wait=[("scalar", ev)])
    p.barrier()
    ar.close()


def bass_diag36(t):
    ap = t[:]
    return bass.AP(ap.tensor, ap.offset,
                   [list(ap.ap[0]), [36, 16], [7, 6]])


def _attn_out(k, b, seq_sc, memv_sc, grow_row, wout_name, bout_name, y_sc):
    p = k.p
    onr = k.c["ones_row_bf"]
    ar = Arena(k.nc, p)
    seqb = ar.sb("ao_seq", [128, 8 * TB], BF16)
    ld = _load_inb(k, seqb, [seq_sc.rearrange("(c p) T -> p c T", p=128)
                             [:, :, b * TB:(b + 1) * TB]])
    if DBG_SUMS2[0] is not None:
        p.op("vector", lambda e: e.memset(
            _ln_scr[0][:, 0:1] if False else seqb[0:1, 0:1], 0.0),
            wait=list(ld)) if False else None
        p.dma("sync", DBG_SUMS2[0][b], seqb[:].rearrange(
            "p (c t) -> p c t", c=8), wait=list(ld))
        p.barrier()
    if memv_sc is not None:
        car = Arena(k.nc, p)
        mvb = car.sb("ao_mv", [128, 8 * TB], BF16)
        tmp = car.sb("ao_tmp", [128, 512])
        g_bcs = [car.sb("ao_gbc", [128, 512]) for _ in range(4)]
        psRs = [car.ps("ao_psR", [128, 512]) for _ in range(2)]
        ld2 = _load_inb(k, mvb, [memv_sc.rearrange("(c p) T -> p c T", p=128)
                                 [:, :, b * TB:(b + 1) * TB]])
        rfree = {}
        for tc in range(4):
            g_bc = g_bcs[tc]
            psR = psRs[tc % 2]
            w = list(ld) + list(ld2)
            if (tc % 2) in rfree:
                w.append(rfree[tc % 2])
            mmr = p.op("tensor", lambda e, tc=tc, psR=psR: e.matmul(
                out=psR[:], lhsT=onr[0:1, :],
                rhs=grow_row[0:1, tc * 512:(tc + 1) * 512],
                start=True, stop=True), wait=w)
            evr = p.op("scalar", lambda e, g_bc=g_bc, psR=psR: e.copy(
                out=g_bc[:], in_=psR[:]), wait=[("tensor", mmr)])
            rfree[tc % 2] = ("scalar", evr)
            if DBG_GBC[0] is not None:
                p.barrier()
                p.dma("sync", DBG_GBC[0][b, tc], g_bc[:])
                p.barrier()
            for dc in range(8):
                p.op("vector", lambda e, dc=dc, tc=tc, g_bc=g_bc: e.tensor_mul(
                    tmp[:], mvb[:, dc * TB + tc * 512: dc * TB + tc * 512 + 512],
                    g_bc[:]), wait=[("scalar", evr)])
                p.op("vector", lambda e, dc=dc, tc=tc: e.tensor_add(
                    seqb[:, dc * TB + tc * 512: dc * TB + tc * 512 + 512],
                    tmp[:],
                    seqb[:, dc * TB + tc * 512: dc * TB + tc * 512 + 512]))
        if DBG_MVB[0] is not None:
            p.dma("sync", DBG_MVB[0][b], mvb[:].rearrange(
                "p (c t) -> p c t", c=8), wait=[("vector", p.n("vector"))])
        p.barrier()
        if DBG_SUMS[0] is not None:
            p.dma("sync", DBG_SUMS[0][b], seqb[:].rearrange(
                "p (c t) -> p c t", c=8))
            p.barrier()
        car.close()
    wt = [ar.sb("ao_w", [128, 8 * 128], BF16) for _ in range(2)]
    stage = [ar.sb("ao_stg", [128, TB], BF16) for _ in range(2)]
    ps = [ar.ps("ao_ps", [128, 512]) for _ in range(8)]
    bufs = (seqb, wt, None, stage, ps)
    _fm_chain_layer(k, ar, bufs, wout_name, bout_name, 8, list(range(8)),
                    lambda c, b=b: _fm_rows(y_sc, c, b), "bias",
                    list(ld) + [("vector", p.n("vector"))])
    ar.close()


def ph_l0attn(k, xnT, memv_sc, seq_sc, y_sc):
    p = k.p
    oar = Arena(k.nc, p)
    wproj_sb = oar.sb("la_wproj", [128, 8 * 272], BF16)
    wv_sb = oar.sb("la_wv", [128, 8 * 1024], BF16)
    bq_sc = oar.sb("la_bq", [128, 8])
    grow_row = oar.sb("la_grow", [1, TB], BF16)
    p.dma("gpsimd", wproj_sb[:].rearrange("p (kc x) -> p kc x", kc=8),
          k.t["w_l0proj"].rearrange("(kc p) x -> p kc x", p=128))
    p.dma("gpsimd", wv_sb[:].rearrange("p (kc x) -> p kc x", kc=8),
          k.t["w_qkv_v"].rearrange("(kc p) x -> p kc x", p=128))
    p.op("vector", lambda e: e.tensor_scalar_mul(
        bq_sc[:], k.c["b_qkv_col"][:, 0:8], 0.125),
        wait=[("sync", p.n("sync"))])
    p.barrier()
    for b in range(NB):
        ar = Arena(k.nc, p)
        inb = ar.sb("la_in", [128, 8 * (TB + 1)], BF16)
        qsb = ar.sb("la_q", [128, 8 * TB], BF16)
        ksb = ar.sb("la_k", [128, 8 * TB], BF16)
        vsb = ar.sb("la_v", [128, 16 * 1024], BF16)
        p.dma("sync", inb[:].rearrange("p (c t) -> p c t", c=8),
              xnT[b].rearrange("(c p) T -> p c T", p=128))
        mar = Arena(k.nc, p)
        mwt = [mar.sb("la_mw", [128, 8 * 128], BF16) for _ in range(2)]
        mstage = [mar.sb("la_mst", [128, TB], BF16) for _ in range(2)]
        mps = [mar.ps("la_mps", [128, 512]) for _ in range(8)]
        _fm_chain_layer(k, mar, (inb, mwt, None, mstage, mps), "w_memv",
                        "b_memv_col", 8, list(range(8)),
                        lambda c, b=b: _fm_rows(memv_sc, c, b), "bias",
                        [("sync", p.n("sync"))], in_stride=TB + 1, in_off=1)
        mar.close()
        _attn_proj(k, b, ar, inb, qsb, ksb, vsb, "w_qkv", wv_sb,
                   k.c["b_qkv_v_row"][0:1, :], bq_sc, k.c["b_qkv_col"][:, 8:16])
        ph_attn_core(k, 0, b, ar, inb, qsb, ksb, vsb, seq_sc)
        _mem_path(k, b, inb, wproj_sb, grow_row[0:1, :])
        if DBG_GROW[0] is not None:
            p.barrier()
            p.dma("sync", DBG_GROW[0][b], grow_row[0:1, :])
            p.barrier()
        ar.close()
        _attn_out(k, b, seq_sc, memv_sc, grow_row[0:1, :], "w_l0out",
                  "b_l0out_col", y_sc)
    p.barrier()
    oar.close()


def ph_l1attn(k, xnT, seq_sc, y_sc):
    p = k.p
    oar = Arena(k.nc, p)
    wv_sb = oar.sb("lb_wv", [128, 8 * 1024], BF16)
    p.dma("gpsimd", wv_sb[:].rearrange("p (kc x) -> p kc x", kc=8),
          k.t["w_l1in_v"].rearrange("(kc p) x -> p kc x", p=128))
    p.barrier()
    for b in range(NB):
        ar = Arena(k.nc, p)
        inb = ar.sb("lb_in", [128, 8 * (TB + 1)], BF16)
        qsb = ar.sb("lb_q", [128, 8 * TB], BF16)
        ksb = ar.sb("lb_k", [128, 8 * TB], BF16)
        vsb = ar.sb("lb_v", [128, 16 * 1024], BF16)
        p.dma("sync", inb[:].rearrange("p (c t) -> p c t", c=8),
              xnT[b].rearrange("(c p) T -> p c T", p=128))
        _attn_proj(k, b, ar, inb, qsb, ksb, vsb, "w_l1in", wv_sb,
                   None, None, None)
        ph_attn_core(k, 1, b, ar, inb, qsb, ksb, vsb, seq_sc)
        ar.close()
        _attn_out(k, b, seq_sc, None, None, "w_l1out", None, y_sc)
    p.barrier()
    oar.close()


L0ATTN_W = ["w_qkv", "b_qkv_col", "w_qkv_v", "b_qkv_v_row", "w_l0proj",
            "b_memg_tile", "memscale_tile", "w_memv", "b_memv_col",
            "w_l0out", "b_l0out_col"]
L1ATTN_W = ["w_l1in", "w_l1in_v", "w_l1out"]


def build_l0attn_launch(host, dbg=False):
    nc = bass.Bass("TRN2", target_bir_lowering=False, debug=False)
    k = K(nc, host)
    kd = "ExternalOutput" if dbg else "Internal"
    x_in = k.dram("x_in", [TOK, D], F32, kind="ExternalInput")
    x_out = k.dram("x_out", [TOK, D], F32, kind="ExternalOutput")
    k.declare_weights([w for w in L0ATTN_W if w not in
                       ("b_qkv_v_row",)])
    xnT = [k.dram(f"xnT{b}", [D, TB + 1], BF16, kind=kd) for b in range(NB)]
    memv_sc = k.dram("memv_sc", [D, TOK], BF16, kind=kd)
    seq_sc = k.dram("seq_sc", [D, TOK], BF16, kind=kd)
    y_sc = k.dram("y_sc", [D, TOK], BF16, kind=kd)
    ph_consts(k, ["ident_bf", "ident_f32", "ones_row_bf", "cmask", "cmaskT",
                  "E5", "E5T", "pfx_bf", "wdn_col", "gcol_bf", "tloc_col",
                  "b_qkv_col", "b_memv_col", "b_l0out_col", "b_memg_tile",
                  "memscale_tile", "b_qkv_v_row"])
    ph_preln(k, x_in, xnT)
    ph_l0attn(k, xnT, memv_sc, seq_sc, y_sc)
    ph_post(k, y_sc, False, x_in, x_out)
    k.p.finalize()
    return k


def build_l1attn_launch(host, dbg=False):
    nc = bass.Bass("TRN2", target_bir_lowering=False, debug=False)
    k = K(nc, host)
    kd = "ExternalOutput" if dbg else "Internal"
    x_in = k.dram("x_in", [TOK, D], F32, kind="ExternalInput")
    x_out = k.dram("x_out", [TOK, D], F32, kind="ExternalOutput")
    k.declare_weights(L1ATTN_W[0:1] + L1ATTN_W[2:3])
    xnT = [k.dram(f"xnT{b}", [D, TB + 1], BF16, kind=kd) for b in range(NB)]
    seq_sc = k.dram("seq_sc", [D, TOK], BF16, kind=kd)
    y_sc = k.dram("y_sc", [D, TOK], BF16, kind=kd)
    ph_consts(k, ["ident_bf", "ident_f32", "ones_row_bf", "cmask", "cmaskT",
                  "E5", "E5T"])
    k.dram("w_l1in_v", host["w_l1in_v"].shape, F32, kind="ExternalInput")
    ph_preln(k, x_in, xnT)
    ph_l1attn(k, xnT, seq_sc, y_sc)
    ph_post(k, y_sc, False, x_in, x_out)
    k.p.finalize()
    return k


def kernel_unfused(**inputs):
    host = prep_host(inputs)
    xs = _shard_x(inputs)
    cores = list(range(NCORES))

    k1 = build_l0attn_launch(host)
    res = run_bass_kernel_spmd(
        k1.nc, [k1.in_map({"x_in": xs[c]}) for c in cores], core_ids=cores)
    xs = [np.asarray(res.results[c]["x_out"], np.float32) for c in cores]

    k2, _ = build_l0ffn_launch(host)
    res = run_bass_kernel_spmd(
        k2.nc, [k2.in_map({"x_in": xs[c]}) for c in cores], core_ids=cores)
    xs = [np.asarray(res.results[c]["x_out"], np.float32) for c in cores]

    k3 = build_l1attn_launch(host)
    res = run_bass_kernel_spmd(
        k3.nc, [k3.in_map({"x_in": xs[c]}) for c in cores], core_ids=cores)
    xs = [np.asarray(res.results[c]["x_out"], np.float32) for c in cores]

    k4 = build_moe_launch(host)
    res = run_bass_kernel_spmd(
        k4.nc, [k4.in_map({"x_in": xs[c]}) for c in cores], core_ids=cores)
    xs = [np.asarray(res.results[c]["x_out"], np.float32) for c in cores]

    out = np.stack([x.reshape(NB, TB, D) for x in xs])
    return np.ascontiguousarray(out.reshape(16, TB, D))


def build_fused(host):
    nc = bass.Bass("TRN2", target_bir_lowering=False, debug=False)
    k = K(nc, host)
    x_in = k.dram("x_in", [TOK, D], F32, kind="ExternalInput")
    x_out = k.dram("x_out", [TOK, D], F32, kind="ExternalOutput")
    k.declare_weights([w for w in L0ATTN_W if w != "b_qkv_v_row"])
    k.declare_weights(L0FFN_W)
    k.declare_weights(["w_l1in", "w_l1out"])
    k.declare_weights(MOE_W)
    k.dram("w_l1in_v", host["w_l1in_v"].shape, F32, kind="ExternalInput")
    x_res = k.dram("x_res", [TOK, D], F32)
    xnT = [k.dram(f"xnT{b}", [D, TB + 1], BF16) for b in range(NB)]
    memv_sc = k.dram("memv_sc", [D, TOK], BF16)
    seq_sc = k.dram("seq_sc", [D, TOK], BF16)
    y_sc = k.dram("y_sc", [D, TOK], BF16)
    y32 = k.dram("y32", [D, TOK], F32)
    h_sc = k.dram("h_sc", [4096, TOK], BF16)
    h2_sc = k.dram("h2_sc", [4096, TOK], BF16)
    h3_sc = k.dram("h3_sc", [4096, TOK], BF16)
    ph_consts(k, ["ident_bf", "ident_f32", "ones_row_bf", "cmask", "cmaskT",
                  "E5", "E5T", "pfx_bf", "wdn_col", "gcol_bf", "tloc_col",
                  "b_qkv_col", "b_memv_col", "b_l0out_col", "b_memg_tile",
                  "memscale_tile", "b_qkv_v_row",
                  "s_f12_col", "s_fh1_col", "s_fh2_col", "s_f3_col"])
    ph_preln(k, x_in, xnT)
    ph_l0attn(k, xnT, memv_sc, seq_sc, y_sc)
    ph_post(k, y_sc, False, x_in, x_res)
    ph_preln(k, x_res, xnT)
    ph_l0ffn(k, xnT, h_sc, h2_sc, h3_sc, y_sc)
    ph_post(k, y_sc, False, x_res, x_res)
    ph_preln(k, x_res, xnT)
    ph_l1attn(k, xnT, seq_sc, y_sc)
    ph_post(k, y_sc, False, x_res, x_res)
    ph_preln(k, x_res, xnT)
    ph_moe(k, xnT, y32)
    ph_post(k, y32, True, x_res, x_out)
    k.p.finalize()
    return k


def kernel(**inputs):
    host = prep_host(inputs)
    xs = _shard_x(inputs)
    cores = list(range(NCORES))
    kk = build_fused(host)
    res = run_bass_kernel_spmd(
        kk.nc, [kk.in_map({"x_in": xs[c]}) for c in cores], core_ids=cores)
    xs = [np.asarray(res.results[c]["x_out"], np.float32) for c in cores]
    out = np.stack([x.reshape(NB, TB, D) for x in xs])
    return np.ascontiguousarray(out.reshape(16, TB, D))


# version 04
# speedup vs baseline: 1.2455x; 1.2455x over previous
import numpy as np
import ml_dtypes
import concourse.bass as bass
import concourse.mybir as mybir
from concourse.bass_utils import run_bass_kernel_spmd

F32 = mybir.dt.float32
BF16 = mybir.dt.bfloat16
AF = mybir.ActivationFunctionType
ALU = mybir.AluOpType
AX = mybir.AxisListType
BF = ml_dtypes.bfloat16

ENG = ("sync", "gpsimd", "vector", "scalar", "tensor")
D = 1024
TB = 2048
NB = 2
TOK = TB * NB
NCORES = 8
H = 16
DH = 64
DECAY = np.float32(0.99)
NEG = -1e30


class P:
    def __init__(self, nc):
        self.nc = nc
        self.streams = {e: [] for e in ENG}
        self._sems = {}
        self._ctx = []
        self._pending = {e: [] for e in ENG}
        self._uid = 0
        self.phase = "init"

    def uid(self):
        self._uid += 1
        return self._uid

    def sem(self, name):
        if name not in self._sems:
            cm = self.nc.semaphore(name)
            h = cm.__enter__()
            self._ctx.append(cm)
            self._sems[name] = [h, 0]
        return self._sems[name]

    def n(self, name):
        return self.sem(name)[1]

    def op(self, eng, fn, wait=(), dma=False):
        s = self.sem(eng)
        delta = 16 if dma else 1
        s[1] += delta
        w = list(self._pending[eng]) + list(wait)
        self._pending[eng] = []
        self.streams[eng].append((w, fn, [(eng, delta)], self.phase))
        return s[1]

    def dma(self, q, out, in_, wait=(), slow=False):
        if slow:
            return self.op(q, lambda e: e.dma_start(
                out=out, in_=in_, allow_slow_non_contiguous=True), wait, dma=True)
        return self.op(q, lambda e: e.dma_start(out=out, in_=in_), wait, dma=True)

    def barrier(self):
        snap = [(e, self._sems[e][1]) for e in ENG
                if e in self._sems and self._sems[e][1] > 0]
        for e in ENG:
            self._pending[e] = list(snap)

    def finalize(self):
        sems = self._sems
        self_chain = ("vector", "scalar", "gpsimd")

        def mk(name):
            items = self.streams[name]
            chain = name in self_chain

            def run(eng):
                prev = 0
                cur = [None]
                import contextlib
                stack = contextlib.ExitStack()
                for waits, fn, incs, ph in items:
                    if ph != cur[0]:
                        stack.close()
                        stack = contextlib.ExitStack()
                        stack.enter_context(self.nc.named_scope(ph))
                        cur[0] = ph
                    m = {}
                    for sname, val in waits:
                        if val > 0:
                            m[sname] = max(m.get(sname, 0), val)
                    if chain and prev > 0:
                        m[name] = max(m.get(name, 0), prev)
                    for sname, val in m.items():
                        eng.wait_ge(sems[sname][0], val)
                    ins = fn(eng)
                    for sname, delta in incs:
                        ins.then_inc(sems[sname][0], delta)
                        if sname == name:
                            prev += delta
                stack.close()
            return run

        with self.nc.Block() as block:
            for e in ENG:
                if self.streams[e]:
                    getattr(block, e)(mk(e))


class Arena:
    def __init__(self, nc, p):
        self.nc = nc
        self.p = p
        self.cms = []

    def sb(self, name, shape, dt=F32):
        cm = self.nc.sbuf_tensor(f"{name}_{self.p.uid()}", list(shape), dt)
        h = cm.__enter__()
        self.cms.append(cm)
        return h

    def ps(self, name, shape, dt=F32):
        cm = self.nc.psum_tensor(f"{name}_{self.p.uid()}", list(shape), dt)
        h = cm.__enter__()
        self.cms.append(cm)
        return h

    def close(self):
        for cm in reversed(self.cms):
            cm.__exit__(None, None, None)
        self.cms = []


def _tile_lhsT(w):
    K, O = w.shape
    return np.ascontiguousarray(
        w.reshape(K // 128, 128, O // 128, 128).transpose(2, 1, 0, 3)
        .reshape(O // 128, 128, (K // 128) * 128))


def prep_host(inputs):
    c = {}
    f = np.float32

    c["w_qkv"] = _tile_lhsT(inputs["L0_attn_qkv_w"].astype(f))
    c["b_qkv_col"] = np.ascontiguousarray(
        inputs["L0_attn_qkv_b"].astype(f).reshape(24, 128).T)
    c["w_qkv_v"] = np.ascontiguousarray(inputs["L0_attn_qkv_w"][:, 2048:3072].astype(f))
    c["b_qkv_v_row"] = np.ascontiguousarray(
        inputs["L0_attn_qkv_b"][2048:3072].astype(f).reshape(1, 1024))
    c["w_l0proj"] = np.ascontiguousarray(np.concatenate(
        [inputs["L0_attn_w1w"], inputs["L0_attn_w2w"], inputs["L0_attn_r1w"],
         inputs["L0_attn_r2w"], inputs["L0_attn_memg_w"]], axis=1).astype(f))
    c["b_memg_tile"] = np.ascontiguousarray(
        np.tile(inputs["L0_attn_memg_b"].astype(f).reshape(1, 16), (128, 1)))
    c["memscale_tile"] = np.ascontiguousarray(
        np.tile(inputs["L0_attn_mem_scale"].astype(f).reshape(1, 16), (128, 1)))
    c["w_memv"] = _tile_lhsT(inputs["L0_attn_memv_w"].astype(f))
    c["b_memv_col"] = np.ascontiguousarray(
        inputs["L0_attn_memv_b"].astype(f).reshape(8, 128).T)
    c["w_l0out"] = _tile_lhsT(inputs["L0_attn_out_w"].astype(f))
    c["b_l0out_col"] = np.ascontiguousarray(
        inputs["L0_attn_out_b"].astype(f).reshape(8, 128).T)

    w12 = np.concatenate([inputs["L0_ffn_w1_w"].T, inputs["L0_ffn_w2_w"].T],
                         axis=1).astype(f)
    c["w_f12"] = _tile_lhsT(w12)
    c["s_f12_col"] = np.ascontiguousarray(np.concatenate(
        [inputs["L0_ffn_w1_s"], inputs["L0_ffn_w2_s"]]).astype(f).reshape(64, 128).T)
    c["w_fh1"] = _tile_lhsT(inputs["L0_ffn_h1_w"].T.astype(f))
    c["s_fh1_col"] = np.ascontiguousarray(
        inputs["L0_ffn_h1_s"].astype(f).reshape(32, 128).T)
    c["w_fh2"] = _tile_lhsT(inputs["L0_ffn_h2_w"].T.astype(f))
    c["s_fh2_col"] = np.ascontiguousarray(
        inputs["L0_ffn_h2_s"].astype(f).reshape(32, 128).T)
    c["w_f3"] = _tile_lhsT(inputs["L0_ffn_w3_w"].T.astype(f))
    c["s_f3_col"] = np.ascontiguousarray(
        inputs["L0_ffn_w3_s"].astype(f).reshape(8, 128).T)

    c["w_l1in"] = _tile_lhsT(inputs["L1_attn_w_in"].T.astype(f))
    c["w_l1in_v"] = np.ascontiguousarray(inputs["L1_attn_w_in"].T[:, 2048:3072].astype(f))
    c["w_l1out"] = _tile_lhsT(inputs["L1_attn_w_out"].T.astype(f))

    c["wg"] = np.ascontiguousarray(inputs["L1_ffn_Wg"].astype(f))
    W1 = inputs["L1_ffn_W1"].astype(f)
    W3 = inputs["L1_ffn_W3"].astype(f)
    W2 = inputs["L1_ffn_W2"].astype(f)
    c["w_e1"] = np.ascontiguousarray(
        W1.reshape(32, 8, 128, 4, 128).transpose(0, 2, 3, 1, 4)
        .reshape(32, 128, 4 * 8 * 128))
    c["w_e3"] = np.ascontiguousarray(
        W3.reshape(32, 8, 128, 4, 128).transpose(0, 2, 3, 1, 4)
        .reshape(32, 128, 4 * 8 * 128))
    c["w_e2"] = np.ascontiguousarray(
        W2.reshape(32, 4, 128, 8, 128).transpose(0, 2, 3, 1, 4)
        .reshape(32, 128, 8 * 4 * 128))
    c["w_s1"] = _tile_lhsT(inputs["L1_ffn_Ws1"].astype(f))
    c["w_s3"] = _tile_lhsT(inputs["L1_ffn_Ws3"].astype(f))
    c["w_s2"] = _tile_lhsT(inputs["L1_ffn_Ws2"].astype(f))

    c["ident_bf"] = np.eye(128, dtype=BF)
    c["ident_f32"] = np.eye(128, dtype=f)
    c["ones_row_bf"] = np.ones((1, 128), BF)
    i_ = np.arange(128)
    cm = np.zeros((128, 128), f)
    cm[i_[:, None] < i_[None, :]] = NEG
    c["cmask"] = cm
    c["cmaskT"] = np.ascontiguousarray(cm.T)
    E5 = np.zeros((128, 640), f)
    for j in range(5):
        delta = i_[:, None] - i_[None, :] + (4 - j) * 128
        Et = delta.astype(f)
        Et[(delta < 0) | (delta > 512)] = NEG
        E5[:, j * 128:(j + 1) * 128] = Et
    c["E5"] = E5
    E5T = np.zeros((128, 640), f)
    for d in range(5):
        E5T[:, d * 128:(d + 1) * 128] = E5[:, (4 - d) * 128:(5 - d) * 128].T
    c["E5T"] = E5T
    c["pfx_bf"] = np.triu(np.ones((128, 128), f), 1).astype(BF)
    c["wdn_col"] = (DECAY ** (-i_.astype(f))).reshape(128, 1).astype(f)
    c["gcol_bf"] = (DECAY ** (128 - i_.astype(f))).reshape(128, 1).astype(BF)
    c["tloc_col"] = (DECAY ** (i_.astype(f))).reshape(128, 1).astype(f)
    return c


WSPEC = {
    "w_qkv": F32, "b_qkv_col": F32, "w_qkv_v": F32, "b_qkv_v_row": F32,
    "w_l0proj": F32, "b_memg_tile": F32, "memscale_tile": F32,
    "w_memv": F32, "b_memv_col": F32, "w_l0out": F32, "b_l0out_col": F32,
    "w_f12": F32, "s_f12_col": F32, "w_fh1": F32, "s_fh1_col": F32,
    "w_fh2": F32, "s_fh2_col": F32, "w_f3": F32, "s_f3_col": F32,
    "w_l1in": F32, "w_l1in_v": F32, "w_l1out": F32,
    "wg": F32, "w_e1": F32, "w_e3": F32, "w_e2": F32,
    "w_s1": F32, "w_s3": F32, "w_s2": F32,
    "ident_bf": BF16, "ident_f32": F32, "ones_row_bf": BF16,
    "cmask": F32, "cmaskT": F32, "E5": F32, "E5T": F32, "pfx_bf": BF16,
    "wdn_col": F32, "gcol_bf": BF16, "tloc_col": F32,
}

PAIRS = [(0, 1), (0, 2), (0, 3), (1, 2), (1, 3), (2, 3)]
SIGN6 = [1.0, -1.0, 1.0, 1.0, -1.0, 1.0]


class K:
    def __init__(self, nc, host_arrays):
        self.nc = nc
        self.p = P(nc)
        self.t = {}
        self.c = {}
        self.host = host_arrays
        self.persist = Arena(nc, self.p)
        self.ext = []

    def dram(self, name, shape, dt, kind="Internal"):
        h = self.nc.dram_tensor(name, list(shape), dt, kind=kind)
        self.t[name] = h.ap()
        if kind == "ExternalInput" and name in self.host:
            self.ext.append(name)
        return self.t[name]

    def declare_weights(self, names):
        for nm in names:
            if nm in self.t:
                continue
            arr = self.host[nm]
            self.dram(nm, arr.shape, WSPEC[nm], kind="ExternalInput")

    def in_map(self, extra):
        m = {nm: self.host[nm] for nm in self.ext}
        m.update(extra)
        return m


def ph_consts(k, names):
    p = k.p
    for nm in names:
        to_bf = SMALL_SB[nm]
        if nm not in k.t:
            k.dram(nm, k.host[nm].shape, WSPEC[nm], kind="ExternalInput")
        ap = k.t[nm]
        shape = list(ap.shape)
        dt = BF16 if to_bf else WSPEC[nm]
        h = k.persist.sb("c_" + nm, shape, dt)
        q = "gpsimd" if (to_bf and WSPEC[nm] == F32) else "sync"
        p.dma(q, h[:], ap)
        k.c[nm] = h
    p.barrier()


def _ln_stats(p, st, src_ap, wait_src, war_wait):
    p.op("vector", lambda e: e.reduce_sum(out=st[:, 0:1], in_=src_ap, axis=AX.X),
         wait=wait_src + war_wait)
    a_ss = p.op("scalar", lambda e: e.activation(
        out=_ln_scr[0][:], in_=src_ap,
        func=AF.Square, accum_out=st[:, 1:2]), wait=wait_src + war_wait)
    p.op("vector", lambda e: e.tensor_scalar_mul(st[:, 2:3], st[:, 0:1], -1.0 / D))
    p.op("vector", lambda e: e.tensor_mul(st[:, 3:4], st[:, 2:3], st[:, 2:3]))
    p.op("vector", lambda e: e.scalar_tensor_tensor(
        out=st[:, 4:5], in0=st[:, 1:2], scalar=1.0 / D, in1=st[:, 3:4],
        op0=ALU.mult, op1=ALU.subtract), wait=[("scalar", a_ss)])
    v_var = p.op("vector", lambda e: e.tensor_scalar_add(st[:, 4:5], st[:, 4:5],
                                                         1e-5))
    a_sq = p.op("scalar", lambda e: e.activation(
        out=st[:, 5:6], in_=st[:, 4:5], func=AF.Sqrt),
        wait=[("vector", v_var)])
    v_inv = p.op("vector", lambda e: e.reciprocal(st[:, 6:7], st[:, 5:6]),
                 wait=[("scalar", a_sq)])
    return v_inv


_ln_scr = [None]


def ph_preln(k, x_src, xnT):
    nc, p = k.nc, k.p
    ar = Arena(nc, p)
    xt = [ar.sb("pl_x", [128, D]) for _ in range(2)]
    _ln_scr[0] = ar.sb("pl_scr", [128, D])
    st = [ar.sb("pl_st", [128, 8]) for _ in range(2)]
    xn = [ar.sb("pl_xn", [128, D], BF16) for _ in range(2)]
    stage = [ar.sb("pl_stg", [128, 8 * 512], BF16) for _ in range(2)]
    zpad = ar.sb("pl_zpad", [128, 8], BF16)
    pt = [ar.ps("pl_pt", [128, 128], BF16) for _ in range(2)]

    idb = k.c["ident_bf"]
    zv = p.op("vector", lambda e: e.memset(zpad[:], 0.0))
    for b in range(NB):
        dst = k.t_xnT_pad(xnT, b)
        p.dma("sync", dst, zpad[:, 0:8].unsqueeze(2), wait=[("vector", zv)],
              slow=True)

    xt_rd = [([], []) for _ in range(2)]
    st_rd = [0, 0]
    xn_rd = [0, 0]
    stage_dma = [0, 0]
    pt_evac = [0, 0]
    for i in range(TOK // 128):
        r = i % 2
        g = i % 4
        sg = (i // 4) % 2
        b = i // 16
        il = i % 16
        w = [("vector", xt_rd[r][0]), ("scalar", xt_rd[r][1])] if i >= 2 else []
        ld = p.dma("sync", xt[r][:], x_src[i * 128:(i + 1) * 128, :], wait=w)
        lw = [("sync", ld)]
        war = [("vector", st_rd[r])] if i >= 2 else []
        v_inv = _ln_stats(p, st[r], xt[r][:], lw, war)
        w = [("tensor", xn_rd[r])] if i >= 2 else []
        xn_done = p.op("vector", lambda e, r=r: e.tensor_scalar(
            out=xn[r][:], in0=xt[r][:], scalar1=st[r][:, 2:3],
            scalar2=st[r][:, 6:7], op0=ALU.add, op1=ALU.mult), wait=w)
        st_rd[r] = xn_done
        xt_rd[r] = (xn_done, p.n("scalar"))
        for cc in range(8):
            pr = (i * 8 + cc) % 2
            w = [("vector", xn_done)]
            if pt_evac[pr]:
                w.append(("scalar", pt_evac[pr]))
            mm = p.op("tensor", lambda e, r=r, cc=cc, pr=pr: e.transpose(
                out=pt[pr][:], in_=xn[r][:, cc * 128:(cc + 1) * 128],
                identity=idb[:]), wait=w)
            if cc == 7:
                xn_rd[r] = mm
            ew = [("tensor", mm)]
            if cc == 0 and g == 0 and stage_dma[sg]:
                ew.append(("sync", stage_dma[sg]))
            pt_evac[pr] = p.op("scalar", lambda e, pr=pr, cc=cc, sg=sg, g=g:
                e.copy(out=stage[sg][:, cc * 512 + g * 128: cc * 512 + g * 128 + 128],
                       in_=pt[pr][:]), wait=ew)
        if g == 3:
            dst = k.t_xnT_slab(xnT, b, (il - 3) * 128)
            stage_dma[sg] = p.dma(
                "sync", dst, stage[sg][:].rearrange("p (c t) -> p c t", c=8),
                wait=[("scalar", pt_evac[0]), ("scalar", pt_evac[1])])
    p.barrier()
    ar.close()


def _t_xnT_pad(xnT, b):
    return xnT[b].rearrange("(c p) T -> p c T", p=128)[:, :, 0:1]


def _t_xnT_slab(xnT, b, t0, width=512):
    return xnT[b].rearrange("(c p) T -> p c T", p=128)[:, :, 1 + t0: 1 + t0 + width]


K.t_xnT_pad = staticmethod(_t_xnT_pad)
K.t_xnT_slab = staticmethod(_t_xnT_slab)


def ph_post(k, y_sc, y_f32, x_src, x_dst):
    nc, p = k.nc, k.p
    ar = Arena(nc, p)
    ydt = F32 if y_f32 else BF16
    slab = [ar.sb("po_slab", [128, 8 * 512], ydt) for _ in range(2)]
    ytm = [ar.sb("po_ytm", [128, D], ydt) for _ in range(2)]
    xt = [ar.sb("po_x", [128, D]) for _ in range(2)]
    tmp = ar.sb("po_tmp", [128, D])
    _ln_scr[0] = ar.sb("po_scr", [128, D])
    st = [ar.sb("po_st", [128, 8]) for _ in range(2)]
    pt = [ar.ps("po_pt", [128, 128], ydt) for _ in range(2)]
    ident = k.c["ident_f32"] if y_f32 else k.c["ident_bf"]

    slab_load = [0, 0]
    slab_users = [0, 0]
    pt_evac = [0, 0]
    ytm_rd = [(0, 0) for _ in range(2)]
    st_rd = [0, 0]
    x_stores = [0, 0]
    for i in range(TOK // 128):
        r = i % 2
        g = i % 4
        sg = (i // 4) % 2
        if g == 0:
            w = [("tensor", slab_users[sg])] if slab_users[sg] else []
            src = y_sc.rearrange("(c p) T -> p c T", p=128)[:, :, i * 128:(i + 4) * 128]
            slab_load[sg] = p.dma("sync", slab[sg][:].rearrange(
                "p (c t) -> p c t", c=8), src, wait=w)
        for cc in range(8):
            pr = (i * 8 + cc) % 2
            w = [("sync", slab_load[sg])]
            if pt_evac[pr]:
                w.append(("scalar", pt_evac[pr]))
            if cc == 0 and i >= 2:
                w += [("vector", ytm_rd[r][0]), ("scalar", ytm_rd[r][1])]
            mm = p.op("tensor", lambda e, cc=cc, sg=sg, g=g, pr=pr: e.transpose(
                out=pt[pr][:], in_=slab[sg][:, cc * 512 + g * 128: cc * 512 + g * 128 + 128],
                identity=ident[:]), wait=w)
            if cc == 7:
                slab_users[sg] = mm
            pt_evac[pr] = p.op("scalar", lambda e, pr=pr, r=r, cc=cc: e.copy(
                out=ytm[r][:, cc * 128:(cc + 1) * 128], in_=pt[pr][:]),
                wait=[("tensor", mm)])
        ev = [("scalar", pt_evac[0]), ("scalar", pt_evac[1])]
        war = [("vector", st_rd[r])] if i >= 2 else []
        v_inv = _ln_stats(p, st[r], ytm[r][:], ev, war)
        act_rd = p.n("scalar")
        w = [("sync", x_stores[r])] if x_stores[r] else []
        x_load = p.dma("sync", xt[r][:], x_src[i * 128:(i + 1) * 128, :], wait=w)
        dv = p.op("vector", lambda e, r=r: e.tensor_scalar(
            out=tmp[:], in0=ytm[r][:], scalar1=st[r][:, 2:3], scalar2=st[r][:, 6:7],
            op0=ALU.add, op1=ALU.mult), wait=[("vector", v_inv)])
        st_rd[r] = dv
        ytm_rd[r] = (dv, act_rd)
        va = p.op("vector", lambda e, r=r: e.tensor_add(xt[r][:], xt[r][:], tmp[:]),
                  wait=[("sync", x_load)])
        x_stores[r] = p.dma("sync", x_dst[i * 128:(i + 1) * 128, :], xt[r][:],
                            wait=[("vector", va)])
    p.barrier()
    ar.close()


def _fm_chain_layer(k, ar, bufs, wname, sname, nk, oc_list, dst_fn, kind,
                    in_done, in_stride=TB, in_off=0):
    p = k.p
    inb, wt, sil, stage, ps = bufs
    wdram = k.t[wname]
    scol = k.c[sname] if sname is not None else None
    st = {"w_rd": [0, 0], "ps_free": [None, None], "stage_dma": [0, 0],
          "sil_rd": [0, 0]}
    first_mm_done = False
    for step, oc in enumerate(oc_list):
        r = step % 2
        par = r * 4
        w = [("tensor", st["w_rd"][r])] if st["w_rd"][r] else []
        wdma = p.dma("gpsimd", wt[r][:, 0:nk * 128], wdram[oc], wait=w)
        mw = [("gpsimd", wdma)]
        if not first_mm_done:
            mw += in_done
            first_mm_done = True
        if st["ps_free"][r] is not None:
            mw.append(st["ps_free"][r])
        for kc in range(nk):
            for t4 in range(4):
                p.op("tensor", lambda e, kc=kc, t4=t4, r=r, par=par, nk=nk:
                     e.matmul(out=ps[par + t4][:],
                              lhsT=wt[r][:, kc * 128:(kc + 1) * 128],
                              rhs=inb[:, kc * in_stride + in_off + t4 * 512:
                                      kc * in_stride + in_off + t4 * 512 + 512],
                              start=(kc == 0), stop=(kc == nk - 1)),
                     wait=(mw if kc == 0 and t4 == 0 else []))
        mm_end = p.n("tensor")
        st["w_rd"][r] = mm_end
        sc_ap = scol[:, oc:oc + 1] if scol is not None else None
        is_silu_step = (kind == "glu" and step % 2 == 0)
        is_glu2_step = (kind == "glu" and step % 2 == 1)
        rs = (step // 2) % 2
        evacs = []
        for t4 in range(4):
            mmw = [("tensor", mm_end - (3 - t4))]
            if is_silu_step:
                if t4 == 0 and st["sil_rd"][rs]:
                    mmw.append(("vector", st["sil_rd"][rs]))
                evacs.append(p.op("scalar", lambda e, t4=t4, par=par, rs=rs,
                                  sc_ap=sc_ap:
                    e.activation(out=sil[rs][:, t4 * 512:(t4 + 1) * 512],
                                 in_=ps[par + t4][:], func=AF.Silu,
                                 scale=sc_ap), wait=mmw))
            elif is_glu2_step:
                if t4 == 0 and st["stage_dma"][rs]:
                    mmw.append(("sync", st["stage_dma"][rs]))
                evacs.append(p.op("vector", lambda e, t4=t4, par=par, rs=rs,
                                  sc_ap=sc_ap:
                    e.scalar_tensor_tensor(
                        out=stage[rs][:, t4 * 512:(t4 + 1) * 512],
                        in0=ps[par + t4][:], scalar=sc_ap,
                        in1=sil[rs][:, t4 * 512:(t4 + 1) * 512],
                        op0=ALU.mult, op1=ALU.mult), wait=mmw))
            else:
                fn = AF.Gelu if kind == "gelu" else AF.Copy
                kw = {}
                if kind == "bias":
                    if scol is not None:
                        kw["bias"] = sc_ap
                        fn = AF.Identity
                else:
                    kw["scale"] = sc_ap
                sr = step % 2
                if t4 == 0 and st["stage_dma"][sr]:
                    mmw.append(("sync", st["stage_dma"][sr]))
                evacs.append(p.op("scalar", lambda e, t4=t4, par=par, sr=sr,
                                  fn=fn, kw=kw:
                    e.activation(out=stage[sr][:, t4 * 512:(t4 + 1) * 512],
                                 in_=ps[par + t4][:], func=fn,
                                 **kw), wait=mmw))
        eng = "vector" if is_glu2_step else "scalar"
        st["ps_free"][r] = (eng, evacs[-1])
        if is_silu_step:
            continue
        if is_glu2_step:
            st["sil_rd"][rs] = evacs[-1]
            c_out = oc - len(oc_list) // 2 if False else (step // 2)
            st["stage_dma"][rs] = p.dma("sync", dst_fn(step // 2), stage[rs][:],
                                        wait=[(eng, x) for x in evacs])
        else:
            sr = step % 2
            st["stage_dma"][sr] = p.dma("sync", dst_fn(oc), stage[sr][:],
                                        wait=[(eng, x) for x in evacs])
    p.barrier()


def _load_inb(k, inb, src_slabs):
    p = k.p
    cnt = 0
    for si, ap in enumerate(src_slabs):
        dst = inb[:, si * 8 * TB:(si + 1) * 8 * TB].rearrange(
            "p (c t) -> p c t", c=8)
        cnt = p.dma("sync", dst, ap)
    return [("sync", cnt)]


def _fm_rows(dram, c, b):
    return dram.rearrange("(c p) T -> p c T", p=128)[:, c, b * TB:(b + 1) * TB]


def ph_l0ffn(k, xnT, h_sc, h2_sc, h3_sc, y_sc):
    p = k.p
    for b in range(NB):
        ar = Arena(k.nc, p)
        inb = ar.sb("ff_in", [128, 32 * TB], BF16)
        wt = [ar.sb("ff_w", [128, 32 * 128], BF16) for _ in range(2)]
        sil = [ar.sb("ff_sil", [128, TB], BF16) for _ in range(2)]
        stage = [ar.sb("ff_stg", [128, TB], BF16) for _ in range(2)]
        ps = [ar.ps("ff_ps", [128, 512]) for _ in range(8)]
        bufs = (inb, wt, sil, stage, ps)

        ind = _load_inb(k, inb, [k.t_xnT_slab(xnT, b, 0, TB)])
        order = []
        for cpair in range(32):
            order += [cpair, 32 + cpair]
        _fm_chain_layer(k, ar, bufs, "w_f12", "s_f12_col", 8, order,
                        lambda c, b=b: _fm_rows(h_sc, c, b), "glu", ind)
        hb = h_sc.rearrange("(c p) T -> p c T", p=128)
        ind = _load_inb(k, inb, [
            hb[:, q * 8:(q + 1) * 8, b * TB:(b + 1) * TB] for q in range(4)])
        _fm_chain_layer(k, ar, bufs, "w_fh1", "s_fh1_col", 32, list(range(32)),
                        lambda c, b=b: _fm_rows(h2_sc, c, b), "gelu", ind)
        h2b = h2_sc.rearrange("(c p) T -> p c T", p=128)
        ind = _load_inb(k, inb, [
            h2b[:, q * 8:(q + 1) * 8, b * TB:(b + 1) * TB] for q in range(4)])
        _fm_chain_layer(k, ar, bufs, "w_fh2", "s_fh2_col", 32, list(range(32)),
                        lambda c, b=b: _fm_rows(h3_sc, c, b), "gelu", ind)
        h3b = h3_sc.rearrange("(c p) T -> p c T", p=128)
        ind = _load_inb(k, inb, [
            h3b[:, q * 8:(q + 1) * 8, b * TB:(b + 1) * TB] for q in range(4)])
        _fm_chain_layer(k, ar, bufs, "w_f3", "s_f3_col", 32, list(range(8)),
                        lambda c, b=b: _fm_rows(y_sc, c, b), "copy", ind)
        ar.close()


def ph_precast_experts(k, e_lo, e_hi):
    p = k.p
    if not hasattr(k, "pc_cnt"):
        k.pc_cnt = {}
    for e in range(e_lo, e_hi):
        p.dma("gpsimd", k.t["we1b"][e], k.t["w_e1"][e])
        p.dma("gpsimd", k.t["we3b"][e], k.t["w_e3"][e])
        k.pc_cnt[e] = p.dma("gpsimd", k.t["we2b"][e], k.t["w_e2"][e])


def ph_moe(k, xnT, y_sc32):
    p = k.p
    onr = k.c["ones_row_bf"]
    idb = k.c["ident_bf"]
    for b in range(NB):
        ar = Arena(k.nc, p)
        inb = ar.sb("mo_in", [128, 8 * TB], BF16)
        yacc = ar.sb("mo_y", [128, 8 * TB])
        m_fm = ar.sb("mo_mfm", [32, TB], BF16)
        h_sb = ar.sb("mo_h", [128, 8 * 512], BF16)
        sil = ar.sb("mo_sil", [128, 4 * 512], BF16)
        t1b = ar.sb("mo_t1", [128, 4 * 512], BF16)
        wg_sb = ar.sb("mo_wg", [128, 8 * 32], BF16)
        bank_free = [None] * 8

        in_done = _load_inb(k, inb, [k.t_xnT_slab(xnT, b, 0, TB)])
        p.dma("gpsimd", wg_sb[:], k.t["wg"].rearrange("(kc p) e -> p kc e", p=128))

        gar = Arena(k.nc, p)
        psg = [gar.ps("mo_psg", [128, 32]) for _ in range(2)]
        pst = [gar.ps("mo_pst", [32, 128], BF16) for _ in range(2)]
        gst = [gar.sb("mo_gst", [128, 8]) for _ in range(2)]
        ex = [gar.sb("mo_ex", [128, 32]) for _ in range(2)]
        top8 = [gar.sb("mo_top8", [128, 8]) for _ in range(2)]
        mtm = [gar.sb("mo_mtm", [128, 32], BF16) for _ in range(2)]
        g_free = [None, None]
        t_free = [None, None]
        ex_rd = [0, 0]
        mtm_rd = [0, 0]
        p.barrier()
        for tt in range(TB // 128):
            r = tt % 2
            w = []
            if g_free[r] is not None:
                w.append(g_free[r])
            for kc in range(8):
                p.op("tensor", lambda e, kc=kc, r=r, tt=tt: e.matmul(
                    out=psg[r][:], lhsT=inb[:, kc * TB + tt * 128: kc * TB + tt * 128 + 128],
                    rhs=wg_sb[:, kc * 32:(kc + 1) * 32],
                    start=(kc == 0), stop=(kc == 7)), wait=(w if kc == 0 else []))
            mm_g = p.n("tensor")
            v1 = p.op("vector", lambda e, r=r: e.reduce_max(
                out=gst[r][:, 0:1], in_=psg[r][:], axis=AX.X), wait=[("tensor", mm_g)])
            vneg = p.op("vector", lambda e, r=r: e.tensor_scalar_mul(
                gst[r][:, 1:2], gst[r][:, 0:1], -1.0))
            a1 = p.op("scalar", lambda e, r=r: e.activation(
                out=ex[r][:], in_=psg[r][:], func=AF.Exp, bias=gst[r][:, 1:2],
                accum_out=gst[r][:, 2:3]),
                wait=[("vector", vneg), ("vector", ex_rd[r])])
            g_free[r] = ("scalar", a1)
            p.op("vector", lambda e, r=r: e.reciprocal(
                gst[r][:, 3:4], gst[r][:, 2:3]), wait=[("scalar", a1)])
            p.op("vector", lambda e, r=r: e.max(out=top8[r][:], in_=ex[r][:]))
            p.op("vector", lambda e, r=r: e.tensor_scalar(
                out=mtm[r][:], in0=ex[r][:], scalar1=top8[r][:, 3:4],
                scalar2=gst[r][:, 3:4], op0=ALU.is_ge, op1=ALU.mult),
                wait=[("tensor", mtm_rd[r])])
            vm = p.op("vector", lambda e, r=r: e.tensor_mul(
                mtm[r][:], mtm[r][:], ex[r][:]))
            ex_rd[r] = vm
            w = [("vector", vm)]
            if t_free[r] is not None:
                w.append(t_free[r])
            mm_t = p.op("tensor", lambda e, r=r: e.transpose(
                out=pst[r][:], in_=mtm[r][:], identity=idb[:]), wait=w)
            mtm_rd[r] = mm_t
            ev = p.op("scalar", lambda e, r=r, tt=tt: e.copy(
                out=m_fm[:, tt * 128:(tt + 1) * 128], in_=pst[r][:]),
                wait=[("tensor", mm_t)])
            t_free[r] = ("scalar", ev)
        p.barrier()
        gar.close()

        ps = [ar.ps("mo_ps", [128, 512]) for _ in range(8)]

        sar = Arena(k.nc, p)
        sil_free = [0, 0, 0, 0]
        wsha = sar.sb("mo_wsha", [128, 3 * 8192], BF16)
        for wi, wn in enumerate(["w_s1", "w_s3", "w_s2"]):
            p.dma("gpsimd", wsha[:, wi * 8192:(wi + 1) * 8192].rearrange(
                "p (oc x) -> p oc x", oc=8),
                k.t[wn].rearrange("oc p x -> p oc x"))
        p.barrier()
        for tc in range(4):
            rhs_off = tc * 512
            for r2 in range(2):
                for br in range(8):
                    hg = r2 * 4 + (br % 4)
                    woff = (0 if br < 4 else 8192) + hg * 1024
                    bank = br if br < 4 else br
                    w = [bank_free[br]] if bank_free[br] else []
                    for kc in range(8):
                        p.op("tensor", lambda e, br=br, kc=kc, woff=woff, rhs_off=rhs_off:
                             e.matmul(out=ps[br][:],
                                      lhsT=wsha[:, woff + kc * 128: woff + kc * 128 + 128],
                                      rhs=inb[:, kc * TB + rhs_off: kc * TB + rhs_off + 512],
                                      start=(kc == 0), stop=(kc == 7)),
                             wait=(w if kc == 0 else []))
                mm_end = p.n("tensor")
                for hc in range(4):
                    hg = r2 * 4 + hc
                    a = p.op("scalar", lambda e, hc=hc: e.activation(
                        out=sil[:, hc * 512:(hc + 1) * 512], in_=ps[hc][:],
                        func=AF.Silu),
                        wait=[("tensor", mm_end - (7 - hc) * 8),
                              ("vector", sil_free[hc])])
                    bank_free[hc] = ("scalar", a)
                    v = p.op("vector", lambda e, hc=hc, hg=hg: e.tensor_mul(
                        h_sb[:, hg * 512:(hg + 1) * 512],
                        sil[:, hc * 512:(hc + 1) * 512], ps[4 + hc][:]),
                        wait=[("scalar", a), ("tensor", mm_end - (3 - hc) * 8)])
                    sil_free[hc] = v
                    bank_free[4 + hc] = ("vector", v)
            h_done = p.n("vector")
            for oc in range(8):
                rb = oc % 2
                w = [bank_free[rb]] if bank_free[rb] else []
                w.append(("vector", h_done))
                for kc in range(8):
                    p.op("tensor", lambda e, oc=oc, kc=kc, rb=rb, rhs_off=rhs_off:
                         e.matmul(out=ps[rb][:],
                                  lhsT=wsha[:, 16384 + oc * 1024 + kc * 128: 16384 + oc * 1024 + kc * 128 + 128],
                                  rhs=h_sb[:, kc * 512:(kc + 1) * 512],
                                  start=(kc == 0), stop=(kc == 7)),
                         wait=(w if kc == 0 else []))
                a = p.op("scalar", lambda e, oc=oc, rb=rb, rhs_off=rhs_off: e.copy(
                    out=yacc[:, oc * TB + rhs_off: oc * TB + rhs_off + 512],
                    in_=ps[rb][:]), wait=[("tensor", p.n("tensor"))])
                bank_free[rb] = ("scalar", a)
        p.barrier()
        sar.close()

        ear = Arena(k.nc, p)
        wex = [ear.sb("mo_wex", [128, 12288], BF16) for _ in range(2)]
        mbcf = [ear.sb("mo_mbcf", [128, TB], BF16) for _ in range(2)]
        wex_rd = [0, 0]
        mbcf_rd = [0, 0]
        sil_rd = 0
        h_rd = [0, 0]
        t1_free = [0, 0, 0, 0]
        def mb_dma(e_i):
            re = e_i % 2
            mf = m_fm[e_i:e_i + 1, :]
            mbsrc = bass.AP(mf.tensor, mf.offset,
                            [list(mf.ap[0]), [0, 128], [1, TB]])
            w = [("gpsimd", mbcf_rd[re])] if mbcf_rd[re] else []
            return p.dma("scalar", mbcf[re][:], mbsrc, wait=w)

        def emit_w2(pend):
            ep, tcp, hhp, hvp, wd2p = pend
            rep = ep % 2
            for oc in range(8):
                rb = oc % 2
                w = [bank_free[rb]] if bank_free[rb] else []
                w.append(("vector", hvp))
                w.append(("scalar", wd2p))
                for kc in range(4):
                    p.op("tensor", lambda e, oc=oc, kc=kc, rb=rb, rep=rep,
                         hhp=hhp:
                         e.matmul(out=ps[rb][:],
                                  lhsT=wex[rep][:, 8192 + (oc * 4 + kc) * 128: 8192 + (oc * 4 + kc) * 128 + 128],
                                  rhs=h_sb[:, (hhp * 4 + kc) * 512:(hhp * 4 + kc + 1) * 512],
                                  start=(kc == 0), stop=(kc == 3)),
                         wait=(w if kc == 0 else []))
                wex_rd[rep] = max(wex_rd[rep], p.n("tensor"))
                if oc == 7:
                    h_rd[hhp] = p.n("tensor")
                v = p.op("vector", lambda e, oc=oc, rb=rb, tcp=tcp:
                    e.tensor_add(
                        yacc[:, oc * TB + tcp * 512: oc * TB + tcp * 512 + 512],
                        ps[rb][:],
                        yacc[:, oc * TB + tcp * 512: oc * TB + tcp * 512 + 512]),
                    wait=[("tensor", p.n("tensor"))])
                bank_free[rb] = ("vector", v)

        md_pf = mb_dma(0)
        pend = None
        for e_i in range(32):
            re = e_i % 2
            w = [("tensor", wex_rd[re])] if wex_rd[re] else []
            w.append(("gpsimd", k.pc_cnt[e_i]))
            wd1 = p.dma("gpsimd", wex[re][:, 0:4096], k.t["we1b"][e_i], wait=w)
            wd3 = p.dma("sync", wex[re][:, 4096:8192], k.t["we3b"][e_i],
                        wait=w)
            wd2 = p.dma("scalar", wex[re][:, 8192:12288], k.t["we2b"][e_i],
                        wait=w)
            md = md_pf
            if e_i < 31:
                md_pf = mb_dma(e_i + 1)
            for tc in range(4):
                hh = tc % 2
                for br in range(8):
                    woff = (0 if br < 4 else 4096) + (br % 4) * 1024
                    w = [bank_free[br]] if bank_free[br] else []
                    w.append(("gpsimd", wd1) if br < 4 else ("sync", wd3))
                    for kc in range(8):
                        mm = p.op("tensor", lambda e, br=br, kc=kc, woff=woff,
                                  re=re, tc=tc:
                             e.matmul(out=ps[br][:],
                                      lhsT=wex[re][:, woff + kc * 128: woff + kc * 128 + 128],
                                      rhs=inb[:, kc * TB + tc * 512: kc * TB + tc * 512 + 512],
                                      start=(kc == 0), stop=(kc == 7)),
                             wait=(w if kc == 0 else []))
                mm_end = p.n("tensor")
                if tc == 3:
                    wex_rd[re] = max(wex_rd[re], mm_end)
                for hc in range(4):
                    a = p.op("scalar", lambda e, hc=hc: e.activation(
                        out=sil[:, hc * 512:(hc + 1) * 512], in_=ps[hc][:],
                        func=AF.Silu),
                        wait=[("tensor", mm_end - (7 - hc) * 8),
                              ("gpsimd", sil_rd)])
                    bank_free[hc] = ("scalar", a)
                    g = p.op("gpsimd", lambda e, hc=hc, re=re, tc=tc:
                        e.tensor_mul(
                            t1b[:, hc * 512:(hc + 1) * 512],
                            sil[:, hc * 512:(hc + 1) * 512],
                            mbcf[re][:, tc * 512:(tc + 1) * 512]),
                        wait=[("scalar", a), ("scalar", md),
                              ("vector", t1_free[hc])])
                    v = p.op("vector", lambda e, hc=hc, hh=hh: e.tensor_mul(
                        h_sb[:, (hh * 4 + hc) * 512:(hh * 4 + hc + 1) * 512],
                        t1b[:, hc * 512:(hc + 1) * 512], ps[4 + hc][:]),
                        wait=[("gpsimd", g), ("tensor", mm_end - (3 - hc) * 8),
                              ("tensor", h_rd[hh])])
                    t1_free[hc] = v
                    bank_free[4 + hc] = ("vector", v)
                sil_rd = p.n("gpsimd")
                mbcf_rd[re] = p.n("gpsimd")
                hv = p.n("vector")
                if pend is not None:
                    emit_w2(pend)
                pend = (e_i, tc, hh, hv, wd2)
        emit_w2(pend)
        yv = p.n("vector")
        for oc in range(8):
            p.dma("sync", _fm_rows(y_sc32, oc, b),
                  yacc[:, oc * TB:(oc + 1) * TB], wait=[("vector", yv)])
        p.barrier()
        ear.close()
        ar.close()


MOE_W = ["wg", "w_e1", "w_e3", "w_e2", "w_s1", "w_s3", "w_s2"]


def build_moe_launch(host, dbg=False):
    nc = bass.Bass("TRN2", target_bir_lowering=False, debug=False)
    k = K(nc, host)
    kd = "ExternalOutput" if dbg else "Internal"
    x_in = k.dram("x_in", [TOK, D], F32, kind="ExternalInput")
    x_out = k.dram("x_out", [TOK, D], F32, kind="ExternalOutput")
    k.declare_weights(MOE_W)
    xnT = [k.dram(f"xnT{b}", [D, TB + 1], BF16, kind=kd) for b in range(NB)]
    y_sc32 = k.dram("y_sc32", [D, TOK], F32, kind=kd)
    for mn in ("we1b", "we3b", "we2b"):
        k.t[mn] = k.dram(mn, [32, 128, 4096], BF16, kind="Internal")
    ph_consts(k, ["ident_bf", "ident_f32", "ones_row_bf"])
    ph_precast_experts(k, 0, 32)
    ph_preln(k, x_in, xnT)
    ph_moe(k, xnT, y_sc32)
    ph_post(k, y_sc32, True, x_in, x_out)
    k.p.finalize()
    return k


L0FFN_W = ["w_f12", "s_f12_col", "w_fh1", "s_fh1_col", "w_fh2", "s_fh2_col",
           "w_f3", "s_f3_col"]
SMALL_SB = {
    "ident_bf": False, "ident_f32": False, "ones_row_bf": False,
    "cmask": False, "cmaskT": False, "E5": False, "E5T": False,
    "pfx_bf": False, "wdn_col": False, "gcol_bf": False, "tloc_col": False,
    "s_f12_col": False, "s_fh1_col": False, "s_fh2_col": False,
    "s_f3_col": False,
    "b_qkv_col": False, "b_memv_col": False, "b_l0out_col": False,
    "b_memg_tile": False, "memscale_tile": False,
    "b_qkv_v_row": True,
}


def build_l0ffn_launch(host, dbg=False):
    nc = bass.Bass("TRN2", target_bir_lowering=False, debug=False)
    k = K(nc, host)
    kd = "ExternalOutput" if dbg else "Internal"
    x_in = k.dram("x_in", [TOK, D], F32, kind="ExternalInput")
    x_out = k.dram("x_out", [TOK, D], F32, kind="ExternalOutput")
    k.declare_weights(L0FFN_W)
    xnT = [k.dram(f"xnT{b}", [D, TB + 1], BF16, kind=kd) for b in range(NB)]
    h_sc = k.dram("h_sc", [4096, TOK], BF16, kind=kd)
    h2_sc = k.dram("h2_sc", [4096, TOK], BF16, kind=kd)
    h3_sc = k.dram("h3_sc", [4096, TOK], BF16, kind=kd)
    y_sc = k.dram("y_sc", [D, TOK], BF16, kind=kd)
    ph_consts(k, ["ident_bf", "s_f12_col", "s_fh1_col", "s_fh2_col", "s_f3_col"])
    ph_preln(k, x_in, xnT)
    ph_l0ffn(k, xnT, h_sc, h2_sc, h3_sc, y_sc)
    ph_post(k, y_sc, False, x_in, x_out)
    k.p.finalize()
    return k, L0FFN_W


def _shard_x(inputs):
    x = np.asarray(inputs["L0_attn_x"], np.float32)
    return [np.ascontiguousarray(x[2 * c:2 * c + 2].reshape(TOK, D))
            for c in range(NCORES)]


SLOPES = [float(2.0 ** (-(h + 1) * 0.5)) for h in range(16)]
DBG_ATTN = [None]
DBG_ATTN2 = [None]
DBG_MEM = [None]
DBG_GROW = [None]
DBG_GBC = [None]
DBG_SUMS = [None]
DBG_SUMS2 = [None]
DBG_MVB = [None]
DBG_MEM2 = [None]


def ph_attn_core(k, blk, b, ar, inb, qsb, ksb, vsb, seq_sc):
    p = k.p
    onr = k.c["ones_row_bf"]
    win = (blk == 1)
    aar = Arena(k.nc, p)
    kaug = [aar.sb("at_kaug", [65, TB], BF16) for _ in range(2)]
    qaug = [aar.sb("at_qaug", [65, TB], BF16) for _ in range(2)]
    nmz = [aar.sb("at_nmz", [128, 16]) for _ in range(2)]
    nb_bf = [aar.sb("at_nbbf", [128, 2], BF16) for _ in range(2)]
    zp = [aar.sb("at_zp", [128, 8]) for _ in range(2)]
    rows16 = [aar.sb("at_rows", [16, 128], BF16) for _ in range(2)]
    onescol = aar.sb("at_onec", [128, 1], BF16)
    strip = aar.sb("at_strip", [128, 2048], BF16)
    scr2 = [aar.sb("at_scr2", [128, 512]) for _ in range(2)]
    pstr = [aar.sb("at_pstr", [128, 512], BF16) for _ in range(16)]
    seqh = [aar.sb("at_seqh", [64, TB], BF16) for _ in range(2)]
    rb_sb = [aar.sb("at_rb", [128, 512]) for _ in range(2)]
    zrow = aar.sb("at_zrow", [1, 512])
    rzb = aar.sb("at_rzb", [1, 512], BF16)
    psA = [aar.ps("at_psA", [128, 512]) for _ in range(4)]
    psB = [aar.ps("at_psB", [128, 512]) for _ in range(2)]
    psO = aar.ps("at_psO", [64, 512])
    psR = aar.ps("at_psR", [128, 512])
    bfree = {}

    def bw(key):
        return [bfree[key]] if key in bfree else []

    p.op("vector", lambda e: e.memset(kaug[0][64:65, :], 1.0))
    p.op("vector", lambda e: e.memset(kaug[1][64:65, :], 1.0))
    p.op("vector", lambda e: e.memset(onescol[:], 1.0))
    NQ = TB // 128

    def emit_pass1(h):
        rh = h % 2
        hc, hp = h // 2, (h % 2) * 64
        w = bw(("kaug", rh))
        vk = p.op("gpsimd", lambda e, hc=hc, hp=hp, rh=rh: e.tensor_copy(
            kaug[rh][0:64, :], ksb[hp:hp + 64, hc * TB:(hc + 1) * TB]), wait=w)
        w = bw(("qaug", rh))
        vq = p.op("gpsimd", lambda e, hc=hc, hp=hp, rh=rh: e.tensor_copy(
            qaug[rh][0:64, :], qsb[hp:hp + 64, hc * TB:(hc + 1) * TB]), wait=w)
        for qi in range(NQ):
            first = max(0, qi - 4) if win else 0
            chunks = []
            c0 = first * 128
            total = (qi + 1) * 128 - c0
            while total > 0:
                wd = min(512, total)
                chunks.append((c0, wd))
                c0 += wd
                total -= wd
            qs = qaug[rh][0:64, qi * 128:(qi + 1) * 128]
            mm_end = []
            for ci, (c0, wd) in enumerate(chunks):
                w = [("gpsimd", max(vk, vq))] + bw(("A", ci))
                mm = p.op("tensor", lambda e, ci=ci, c0=c0, wd=wd, qs=qs, rh=rh:
                          e.matmul(out=psA[ci][:, 0:wd], lhsT=qs,
                                   rhs=kaug[rh][0:64, c0:c0 + wd],
                                   start=True, stop=True), wait=w)
                mm_end.append(mm)
            if win:
                sl = SLOPES[h]
                wslice = sum(wd for _, wd in chunks)
                off = 640 - wslice
                for ci, (c0, wd) in enumerate(chunks):
                    pos = off + (c0 - chunks[0][0])
                    v = p.op("vector", lambda e, ci=ci, pos=pos, wd=wd, sl=sl:
                             e.scalar_tensor_tensor(
                                 out=strip[:, pos:pos + wd],
                                 in0=k.c["E5"][:, pos:pos + wd], scalar=sl,
                                 in1=psA[ci][:, 0:wd],
                                 op0=ALU.mult, op1=ALU.add),
                             wait=[("tensor", mm_end[ci])])
                    bfree[("A", ci)] = ("vector", v)
                p.op("vector", lambda e, off=off, wslice=wslice, rh=rh:
                     e.reduce_max(out=zp[rh][:, 4:5],
                                  in_=strip[:, off:off + wslice], axis=AX.X),
                     wait=bw(("zp", rh)))
            else:
                evs = []
                for ci, (c0, wd) in enumerate(chunks):
                    pos = c0 - chunks[0][0]
                    a = p.op("scalar", lambda e, ci=ci, pos=pos, wd=wd:
                             e.copy(out=strip[:, pos:pos + wd],
                                    in_=psA[ci][:, 0:wd]),
                             wait=[("tensor", mm_end[ci])])
                    evs.append(a)
                    bfree[("A", ci)] = ("scalar", a)
                wslice = sum(wd for _, wd in chunks)
                p.op("vector", lambda e, wslice=wslice, rh=rh: e.reduce_max(
                    out=zp[rh][:, 4:5], in_=strip[:, 0:wslice], axis=AX.X),
                    wait=[("scalar", evs[-1])] + bw(("zp", rh)))
            rm = p.n("vector")
            g1 = p.op("gpsimd", lambda e, rh=rh: e.tensor_scalar_mul(
                zp[rh][:, 5:6], zp[rh][:, 4:5], -1.0), wait=[("vector", rm)])
            bfree[("zp", rh)] = ("gpsimd", g1)
            p.op("gpsimd", lambda e, rh=rh: e.tensor_copy(
                nb_bf[rh][:, 0:1], zp[rh][:, 5:6]))
            p.op("gpsimd", lambda e, qi=qi, rh=rh: e.tensor_copy(
                nmz[rh][:, qi:qi + 1], nb_bf[rh][:, 0:1]))
        w = bw(("R", 0)) + bw(("rows", rh))
        mm_t = p.op("tensor", lambda e, rh=rh: e.transpose(
            out=psR[0:16, 0:128], in_=nmz[rh][:],
            identity=k.c["ident_f32"][:]),
            wait=[("gpsimd", p.n("gpsimd"))] + w)
        bfree[("nmz", rh)] = ("tensor", mm_t)
        ev_t = p.op("scalar", lambda e, rh=rh: e.copy(
            out=rows16[rh][:], in_=psR[0:16, 0:128]), wait=[("tensor", mm_t)])
        bfree[("R", 0)] = ("scalar", ev_t)
        d1 = p.dma("sync", qaug[rh][64:65, :], rows16[rh][:],
                   wait=[("scalar", ev_t)])
        bfree[("rows", rh)] = ("sync", d1)
        return vk, vq, d1

    def emit_pass2(h, d1):
        rh = h % 2
        pv_last = 0
        for g in range(NQ // 4):
            ts_lo = max(0, 4 * g - 4) if win else 0
            ts_hi = 4 * g + 3
            exp_done = {}
            for ts in range(ts_lo, ts_hi + 1):
                m0 = max(ts, 4 * g)
                m1 = min(ts + 4, ts_hi) if win else ts_hi
                col0 = m0 * 128
                N = (m1 + 1) * 128 - col0
                pi = ts % 2
                w = [("sync", d1)] + bw(("B", pi))
                mm2 = p.op("tensor", lambda e, ts=ts, col0=col0, N=N, pi=pi,
                           rh=rh:
                           e.matmul(out=psB[pi][:, 0:N],
                                    lhsT=kaug[rh][:, ts * 128:(ts + 1) * 128],
                                    rhs=qaug[rh][:, col0:col0 + N],
                                    start=True, stop=True), wait=w)
                lo = (m0 - 4 * g) * 128
                hi = lo + N
                wz = bw(("pstr", ts))
                if lo > 0:
                    p.op("gpsimd", lambda e, ts=ts, lo=lo: e.memset(
                        pstr[ts][:, 0:lo], 0.0), wait=wz)
                if hi < 512:
                    p.op("gpsimd", lambda e, ts=ts, hi=hi: e.memset(
                        pstr[ts][:, hi:512], 0.0), wait=wz)
                if win:
                    sl = SLOPES[h]
                    doff = (m0 - ts) * 128
                    v2 = p.op("vector", lambda e, pi=pi, N=N, doff=doff, sl=sl:
                              e.scalar_tensor_tensor(
                                  out=scr2[pi][:, 0:N],
                                  in0=k.c["E5T"][:, doff:doff + N], scalar=sl,
                                  in1=psB[pi][:, 0:N],
                                  op0=ALU.mult, op1=ALU.add),
                              wait=[("tensor", mm2)] + bw(("scr2", pi)))
                    bfree[("B", pi)] = ("vector", v2)
                    a2 = p.op("scalar", lambda e, ts=ts, lo=lo, N=N, pi=pi:
                              e.activation(out=pstr[ts][:, lo:lo + N],
                                           in_=scr2[pi][:, 0:N], func=AF.Exp),
                              wait=[("vector", v2)] + wz)
                    bfree[("scr2", pi)] = ("scalar", a2)
                else:
                    wd2 = [("tensor", mm2)]
                    if ts >= 4 * g:
                        vm2 = p.op("vector", lambda e, pi=pi: e.tensor_tensor(
                            out=psB[pi][:, 0:128], in0=psB[pi][:, 0:128],
                            in1=k.c["cmaskT"][:], op=ALU.add),
                            wait=[("tensor", mm2)])
                        wd2.append(("vector", vm2))
                    a2 = p.op("scalar", lambda e, ts=ts, lo=lo, N=N, pi=pi:
                              e.activation(out=pstr[ts][:, lo:lo + N],
                                           in_=psB[pi][:, 0:N], func=AF.Exp),
                              wait=wd2 + wz)
                    bfree[("B", pi)] = ("scalar", a2)
                exp_done[ts] = a2
            nts = ts_hi - ts_lo + 1
            for j, ts in enumerate(range(ts_lo, ts_hi + 1)):
                w = [("scalar", exp_done[ts]), ("vector", p.n("vector")),
                     ("gpsimd", p.n("gpsimd"))]
                if j == 0:
                    w += bw(("O", 0))
                pv = p.op("tensor", lambda e, ts=ts, j=j, nts=nts, h=h:
                          e.matmul(out=psO[:],
                                   lhsT=vsb[:, ts * 1024 + h * 64: ts * 1024 + h * 64 + 64],
                                   rhs=pstr[ts][:, 0:512],
                                   start=(j == 0), stop=(j == nts - 1)), wait=w)
            for j, ts in enumerate(range(ts_lo, ts_hi + 1)):
                w = bw(("R", 0)) if j == 0 else []
                pz = p.op("tensor", lambda e, ts=ts, j=j, nts=nts:
                          e.matmul(out=psR[0:1, :],
                                   lhsT=onescol[:, 0:1],
                                   rhs=pstr[ts][:, 0:512],
                                   start=(j == 0), stop=(j == nts - 1)),
                          wait=w)
                bfree[("pstr", ts)] = ("tensor", pz)
            pv_last = p.n("tensor")
            az = p.op("scalar", lambda e: e.copy(out=zrow[:], in_=psR[0:1, :]),
                      wait=[("tensor", pv_last)] + bw(("zrow", 0)))
            vzb = p.op("gpsimd", lambda e: e.tensor_copy(rzb[:], zrow[:]),
                       wait=[("scalar", az)] + bw(("rzb", 0)))
            bfree[("zrow", 0)] = ("gpsimd", vzb)
            mmr = p.op("tensor", lambda e: e.matmul(
                out=psR[:], lhsT=onr[0:1, :], rhs=rzb[0:1, :],
                start=True, stop=True), wait=[("gpsimd", vzb)])
            bfree[("rzb", 0)] = ("tensor", mmr)
            gr = g % 2
            evr0 = p.op("scalar", lambda e, gr=gr: e.copy(
                out=rb_sb[gr][:], in_=psR[:]),
                wait=[("tensor", mmr)] + bw(("rb", gr)))
            bfree[("R", 0)] = ("scalar", evr0)
            evr = p.op("vector", lambda e, gr=gr: e.reciprocal(
                rb_sb[gr][:], rb_sb[gr][:]), wait=[("scalar", evr0)])
            w = [("vector", evr), ("tensor", pv_last)]
            if g == 0:
                w += bw(("seqh", rh))
            vdiv = p.op("vector", lambda e, g=g, rh=rh, gr=gr: e.tensor_mul(
                seqh[rh][:, g * 512:(g + 1) * 512], psO[:],
                rb_sb[gr][0:64, :]), wait=w)
            bfree[("rb", gr)] = ("vector", vdiv)
            bfree[("O", 0)] = ("vector", vdiv)
        kq_done = p.n("tensor")
        bfree[("kaug", rh)] = ("tensor", kq_done)
        bfree[("qaug", rh)] = ("tensor", kq_done)
        sd = p.dma("sync", seq_sc.rearrange(
            "(c p) T -> p c T", p=64)[:, h, b * TB:(b + 1) * TB],
            seqh[rh][:], wait=[("vector", p.n("vector"))])
        bfree[("seqh", rh)] = ("sync", sd)

    prev = None
    for h in range(H):
        vk, vq, d1 = emit_pass1(h)
        if prev is not None:
            emit_pass2(*prev)
        prev = (h, d1)
    emit_pass2(*prev)
    p.barrier()
    aar.close()


def _attn_proj(k, b, ar, inb, qsb, ksb, vsb, wname, wvname, bv_row, bq_col,
               bk_col):
    p = k.p
    onr = k.c["ones_row_bf"]
    par_ = Arena(k.nc, p)
    wt = [par_.sb("ap_w", [128, 8 * 128], BF16) for _ in range(2)]
    ps = [par_.ps("ap_ps", [128, 512]) for _ in range(8)]
    wv_sb = par_.sb("ap_wv", [128, 8 * 1024], BF16)
    p.dma("gpsimd", wv_sb[:].rearrange("p (kc x) -> p kc x", kc=8),
          k.t[wvname].rearrange("(kc p) x -> p kc x", p=128))
    wdram = k.t[wname]
    st = {"w_rd": [0, 0], "ps_free": [None, None]}
    first = True
    for step, oc in enumerate(range(16)):
        r = step % 2
        par = r * 4
        w = [("tensor", st["w_rd"][r])] if st["w_rd"][r] else []
        wdma = p.dma("gpsimd", wt[r][:], wdram[oc], wait=w)
        mw = [("gpsimd", wdma)]
        if first:
            mw += [("sync", p.n("sync"))]
            first = False
        if st["ps_free"][r] is not None:
            mw.append(st["ps_free"][r])
        for kc in range(8):
            for t4 in range(4):
                p.op("tensor", lambda e, kc=kc, t4=t4, r=r, par=par:
                     e.matmul(out=ps[par + t4][:],
                              lhsT=wt[r][:, kc * 128:(kc + 1) * 128],
                              rhs=inb[:, kc * (TB + 1) + 1 + t4 * 512:
                                      kc * (TB + 1) + 1 + t4 * 512 + 512],
                              start=(kc == 0), stop=(kc == 7)),
                     wait=(mw if kc == 0 and t4 == 0 else []))
        mm_end = p.n("tensor")
        st["w_rd"][r] = mm_end
        dst = qsb if oc < 8 else ksb
        c = oc % 8
        evacs = []
        for t4 in range(4):
            mmw = [("tensor", mm_end - (3 - t4))]
            kw = {}
            if oc < 8:
                kw["scale"] = 0.125
                if bq_col is not None:
                    kw["bias"] = bq_col[:, c:c + 1]
            else:
                if bk_col is not None:
                    kw["bias"] = bk_col[:, c:c + 1]
            fnq = AF.Identity if "bias" in kw else AF.Copy
            evacs.append(p.op("scalar", lambda e, t4=t4, par=par, dst=dst,
                              c=c, kw=kw, fnq=fnq:
                e.activation(out=dst[:, c * TB + t4 * 512: c * TB + t4 * 512 + 512],
                             in_=ps[par + t4][:], func=fnq, **kw),
                wait=mmw))
        st["ps_free"][r] = ("scalar", evacs[-1])
    p.barrier()
    for cm in par_.cms[:2 + 8]:
        pass
    par2 = Arena(k.nc, p)
    vps = None
    vps = [ps[0], ps[1], ps[2], ps[3]]
    v_free = [None, None]
    for tt in range(TB // 128):
        r = tt % 2
        w = []
        if tt == 0:
            w += [x for x in st["ps_free"] if x]
        if v_free[r] is not None:
            w.append(v_free[r])
        nb = 0 if bv_row is None else 1
        for kc in range(8):
            for half in range(2):
                p.op("tensor", lambda e, kc=kc, half=half, r=r, tt=tt:
                     e.matmul(out=vps[r * 2 + half][:],
                              lhsT=inb[:, kc * (TB + 1) + 1 + tt * 128:
                                       kc * (TB + 1) + 1 + tt * 128 + 128],
                              rhs=wv_sb[:, kc * 1024 + half * 512:
                                        kc * 1024 + half * 512 + 512],
                              start=(kc == 0), stop=(kc == 7 and nb == 0)),
                     wait=(w if kc == 0 and half == 0 else []))
        if nb:
            for half in range(2):
                p.op("tensor", lambda e, half=half, r=r: e.matmul(
                    out=vps[r * 2 + half][:], lhsT=onr[0:1, :],
                    rhs=bv_row[0:1, half * 512: half * 512 + 512],
                    start=False, stop=True))
        mm_end = p.n("tensor")
        evacs = []
        for half in range(2):
            evacs.append(p.op("scalar", lambda e, half=half, r=r, tt=tt:
                e.copy(out=vsb[:, tt * 1024 + half * 512: tt * 1024 + half * 512 + 512],
                       in_=vps[r * 2 + half][:]),
                wait=[("tensor", mm_end - (1 - half))]))
        v_free[r] = ("scalar", evacs[-1])
    p.barrier()
    par_.close()


def _mem_path(k, b, inb, wproj_sb, grow_row, in_done=()):
    p = k.p
    onr = k.c["ones_row_bf"]
    ar = Arena(k.nc, p)
    w1s = ar.sb("mp_w1s", [128, 64])
    r1s = ar.sb("mp_r1s", [128, 64])
    pair = ar.sb("mp_pair", [128, 256])
    wl = ar.sb("mp_wl", [128, 96])
    rls = ar.sb("mp_rls", [128, 96])
    jju = ar.sb("mp_jju", [128, 576])
    jjs_bf = ar.sb("mp_jjs", [128, 576], BF16)
    jjw_bf = ar.sb("mp_jjw", [128, 576], BF16)
    rru = ar.sb("mp_rru", [128, 576])
    dt5 = ar.sb("mp_dt5", [128, 576])
    s16 = ar.sb("mp_s16", [128, 8 * 4])
    m16 = ar.sb("mp_m16", [128, 16])
    g16 = ar.sb("mp_g16", [128, 16])
    Cf = ar.sb("mp_C", [1, 576])
    Cbf = ar.sb("mp_Cbf", [1, 576], BF16)
    gcols = ar.sb("mp_gcols", [128, 16])
    grsb = ar.sb("mp_grsb", [16, 128], BF16)
    pw1 = ar.ps("mp_pw1", [128, 64])
    pwrg = ar.ps("mp_pwrg", [128, 208])
    psP = ar.ps("mp_psP", [128, 512])
    psP2 = ar.ps("mp_psP2", [128, 64])
    psG = ar.ps("mp_psG", [1, 512])
    psG2 = ar.ps("mp_psG2", [1, 64])
    pt16 = ar.ps("mp_pt16", [16, 128])
    bfree = {}

    def bw(key):
        return [bfree[key]] if key in bfree else []

    vC = p.op("vector", lambda e: e.memset(Cf[:], 0.0))
    vCb = p.op("vector", lambda e: e.memset(Cbf[:], 0.0))
    cbf_done = vCb
    v6 = lambda t, i: t.rearrange("p (h k) -> p h k", k=6)[:, :, i:i + 1]
    p16 = lambda i: pair.rearrange("p (h k) -> p h k", k=16)[:, :, i:i + 1]
    _mp_st = 9
    for tt in range(TB // 128):
        w = bw("pw1") + (list(in_done) if tt == 0 else [])
        for kc in range(8):
            p.op("tensor", lambda e, kc=kc, tt=tt: e.matmul(
                out=pw1[:], lhsT=inb[:, kc * (TB + 1) + tt * 128:
                                     kc * (TB + 1) + tt * 128 + 128],
                rhs=wproj_sb[:, kc * 272: kc * 272 + 64],
                start=(kc == 0), stop=(kc == 7)),
                wait=(w if kc == 0 else []))
        mm_w1 = p.n("tensor")
        w = bw("pwrg") + (list(in_done) if tt == 0 else [])
        for kc in range(8):
            p.op("tensor", lambda e, kc=kc, tt=tt: e.matmul(
                out=pwrg[:], lhsT=inb[:, kc * (TB + 1) + 1 + tt * 128:
                                      kc * (TB + 1) + 1 + tt * 128 + 128],
                rhs=wproj_sb[:, kc * 272 + 64: kc * 272 + 272],
                start=(kc == 0), stop=(kc == 7)),
                wait=(w if kc == 0 else []))
        mm_wrg = p.n("tensor")
        a_w1 = p.op("scalar", lambda e: e.copy(out=w1s[:], in_=pw1[:]),
                    wait=[("tensor", mm_w1)] + bw("w1s"))
        bfree["pw1"] = ("scalar", a_w1)
        vv = p.op("vector", lambda e: e.tensor_tensor(
            out=pair.rearrange("p (h i j) -> p h i j", i=4, j=4),
            in0=w1s.rearrange("p (h i) -> p h i", i=4).unsqueeze(3)
                .broadcast_to([128, 16, 4, 4]),
            in1=pwrg[:, 0:64].rearrange("p (h j) -> p h j", j=4).unsqueeze(2)
                .broadcast_to([128, 16, 4, 4]),
            op=ALU.mult), wait=[("scalar", a_w1), ("tensor", mm_wrg)])
        bfree["w1s"] = ("vector", vv)
        for kk, (i, j) in enumerate(PAIRS):
            p.op("vector", lambda e, kk=kk, i=i, j=j: e.tensor_sub(
                v6(wl, kk), p16(i * 4 + j), p16(j * 4 + i)))
        a_r1 = p.op("vector", lambda e: e.tensor_copy(r1s[:], pwrg[:, 64:128]),
                    wait=[("tensor", mm_wrg)] + bw("r1s"))
        vr = p.op("vector", lambda e: e.tensor_tensor(
            out=pair.rearrange("p (h i j) -> p h i j", i=4, j=4),
            in0=r1s.rearrange("p (h i) -> p h i", i=4).unsqueeze(3)
                .broadcast_to([128, 16, 4, 4]),
            in1=pwrg[:, 128:192].rearrange("p (h j) -> p h j", j=4).unsqueeze(2)
                .broadcast_to([128, 16, 4, 4]),
            op=ALU.mult))
        bfree["r1s"] = ("vector", vr)
        for kk in range(6):
            a2, b2 = PAIRS[5 - kk]
            if SIGN6[kk] > 0:
                p.op("vector", lambda e, kk=kk, a2=a2, b2=b2: e.tensor_sub(
                    v6(rls, kk), p16(a2 * 4 + b2), p16(b2 * 4 + a2)))
            else:
                p.op("vector", lambda e, kk=kk, a2=a2, b2=b2: e.tensor_sub(
                    v6(rls, kk), p16(b2 * 4 + a2), p16(a2 * 4 + b2)))
        if _mp_st < 2:
            continue
        vjj = p.op("vector", lambda e: e.tensor_tensor(
            out=jju.rearrange("p (h i j) -> p h i j", i=6, j=6),
            in0=wl.rearrange("p (h i) -> p h i", i=6).unsqueeze(3)
                .broadcast_to([128, 16, 6, 6]),
            in1=wl.rearrange("p (h j) -> p h j", j=6).unsqueeze(2)
                .broadcast_to([128, 16, 6, 6]),
            op=ALU.mult))
        p.op("vector", lambda e: e.reduce_sum(
            out=s16[:, 0:16],
            in_=bass_diag36(jju), axis=AX.X))
        p.op("vector", lambda e: e.tensor_scalar_max(
            s16[:, 0:16], s16[:, 0:16], 1e-24))
        viw = p.op("vector", lambda e: e.reciprocal(s16[:, 16:32],
                                                    s16[:, 0:16]))
        vjs = p.op("vector", lambda e: e.tensor_tensor(
            out=jjs_bf.rearrange("p (h x) -> p h x", x=36),
            in0=jju.rearrange("p (h x) -> p h x", x=36),
            in1=s16[:, 16:32].unsqueeze(2).broadcast_to([128, 16, 36]),
            op=ALU.mult), wait=bw("jjs"))
        vjw = p.op("vector", lambda e: e.tensor_scalar(
            out=jjw_bf[:], in0=jjs_bf[:], scalar1=k.c["wdn_col"][:, 0:1],
            scalar2=None, op0=ALU.mult), wait=bw("jjw"))
        vrr = p.op("vector", lambda e: e.tensor_tensor(
            out=rru.rearrange("p (h i j) -> p h i j", i=6, j=6),
            in0=rls.rearrange("p (h i) -> p h i", i=6).unsqueeze(3)
                .broadcast_to([128, 16, 6, 6]),
            in1=rls.rearrange("p (h j) -> p h j", j=6).unsqueeze(2)
                .broadcast_to([128, 16, 6, 6]),
            op=ALU.mult))
        p.op("vector", lambda e: e.reduce_sum(
            out=s16[:, 0:16], in_=bass_diag36(rru), axis=AX.X))
        p.op("vector", lambda e: e.tensor_scalar_max(
            s16[:, 0:16], s16[:, 0:16], 1e-24))
        vir = p.op("vector", lambda e: e.reciprocal(s16[:, 0:16],
                                                    s16[:, 0:16]))
        if _mp_st < 3:
            continue
        w = bw("psP") + [("vector", vjw)]
        mmP = p.op("tensor", lambda e: e.matmul(
            out=psP[:], lhsT=k.c["pfx_bf"][:], rhs=jjw_bf[:, 0:512],
            start=True, stop=False), wait=w + [("vector", cbf_done)])
        mmP = p.op("tensor", lambda e: e.matmul(
            out=psP[:], lhsT=onr[0:1, :], rhs=Cbf[0:1, 0:512],
            start=False, stop=True))
        w = bw("psP2")
        mmP2 = p.op("tensor", lambda e: e.matmul(
            out=psP2[:], lhsT=k.c["pfx_bf"][:], rhs=jjw_bf[:, 512:576],
            start=True, stop=False), wait=w)
        mmP2 = p.op("tensor", lambda e: e.matmul(
            out=psP2[:], lhsT=onr[0:1, :], rhs=Cbf[0:1, 512:576],
            start=False, stop=True))
        w = bw("psG") + [("vector", vjs)]
        mmG = p.op("tensor", lambda e: e.matmul(
            out=psG[:], lhsT=k.c["gcol_bf"][:, 0:1], rhs=jjs_bf[:, 0:512],
            start=True, stop=True), wait=w)
        w = bw("psG2")
        mmG2 = p.op("tensor", lambda e: e.matmul(
            out=psG2[:], lhsT=k.c["gcol_bf"][:, 0:1], rhs=jjs_bf[:, 512:576],
            start=True, stop=True), wait=w)
        vc1 = p.op("vector", lambda e: e.scalar_tensor_tensor(
            out=Cf[0:1, 0:512], in0=Cf[0:1, 0:512], scalar=float(DECAY ** 128),
            in1=psG[:], op0=ALU.mult, op1=ALU.add), wait=[("tensor", mmG)])
        bfree["psG"] = ("vector", vc1)
        vc2 = p.op("vector", lambda e: e.scalar_tensor_tensor(
            out=Cf[0:1, 512:576], in0=Cf[0:1, 512:576], scalar=float(DECAY ** 128),
            in1=psG2[:], op0=ALU.mult, op1=ALU.add), wait=[("tensor", mmG2)])
        bfree["psG2"] = ("vector", vc2)
        cbf_done = p.op("vector", lambda e: e.tensor_copy(Cbf[:], Cf[:]),
                        wait=[("tensor", mmP), ("tensor", mmP2)])
        if _mp_st < 4:
            bfree["psP"] = ("tensor", mmP)
            bfree["psP2"] = ("tensor", mmP2)
            bfree["jjw"] = ("tensor", mmP)
            bfree["jjs"] = ("tensor", mmG2)
            continue
        vd1 = p.op("vector", lambda e: e.tensor_mul(
            dt5[:, 0:512], rru[:, 0:512], psP[:]),
            wait=[("tensor", mmP), ("vector", vrr)])
        bfree["psP"] = ("vector", vd1)
        vd2 = p.op("vector", lambda e: e.tensor_mul(
            dt5[:, 512:576], rru[:, 512:576], psP2[:]), wait=[("tensor", mmP2)])
        bfree["psP2"] = ("vector", vd2)
        bfree["jjw"] = ("tensor", mmP)
        bfree["jjs"] = ("tensor", mmG2)
        p.op("vector", lambda e: e.reduce_sum(
            out=m16[:], in_=dt5.rearrange("p (h x) -> p h x", x=36), axis=AX.X))
        if DBG_MEM[0] is not None:
            p.barrier()
            p.dma("sync", DBG_MEM[0][tt, 0], m16[:], wait=[("vector", p.n("vector"))])
            p.dma("sync", DBG_MEM[0][tt, 1], s16[:, 0:16], wait=[("vector", p.n("vector"))])
            p.barrier()
        if DBG_MEM2[0] is not None:
            p.barrier()
            p.dma("sync", DBG_MEM2[0][tt, 0], rru[:], wait=[("vector", p.n("vector"))])
            p.dma("sync", DBG_MEM2[0][tt, 1], dt5[:], wait=[("vector", p.n("vector"))])
            p.barrier()
        p.op("vector", lambda e: e.tensor_scalar(
            out=s16[:, 0:16], in0=s16[:, 0:16],
            scalar1=k.c["tloc_col"][:, 0:1], scalar2=None, op0=ALU.mult),
            wait=[("vector", vir)])
        p.op("vector", lambda e: e.tensor_mul(m16[:], m16[:], s16[:, 0:16]))
        if DBG_MEM[0] is not None:
            p.barrier()
            p.dma("sync", DBG_MEM[0][tt, 2], m16[:], wait=[("vector", p.n("vector"))])
            p.barrier()
        vmm = p.op("vector", lambda e: e.tensor_mul(
            m16[:], m16[:], k.c["memscale_tile"][:]))
        a_sg = p.op("scalar", lambda e: e.activation(
            out=m16[:], in_=m16[:], func=AF.Sigmoid), wait=[("vector", vmm)])
        vg = p.op("vector", lambda e: e.scalar_tensor_tensor(
            out=g16[:], in0=pwrg[:, 192:208], scalar=1.0,
            in1=k.c["b_memg_tile"][:], op0=ALU.mult, op1=ALU.add),
            wait=[("tensor", mm_wrg)])
        bfree["pwrg"] = ("vector", vg)
        if DBG_MEM[0] is not None:
            p.dma("sync", DBG_MEM[0][tt, 3], g16[:], wait=[("vector", vg)])
        a_g = p.op("scalar", lambda e: e.activation(
            out=g16[:], in_=g16[:], func=AF.Sigmoid), wait=[("vector", vg)])
        p.op("vector", lambda e: e.tensor_mul(m16[:], m16[:], g16[:]),
             wait=[("scalar", a_sg), ("scalar", a_g)])
        p.op("vector", lambda e: e.reduce_sum(
            out=s16[:, 20:21], in_=m16[:], axis=AX.X))
        vgc = p.op("vector", lambda e, tt=tt: e.tensor_scalar_mul(
            gcols[:, tt:tt + 1], s16[:, 20:21], 1.0 / 16.0))
    mm_t = p.op("tensor", lambda e: e.transpose(
        out=pt16[:], in_=gcols[:], identity=k.c["ident_f32"][:]),
        wait=[("vector", p.n("vector"))])
    ev = p.op("scalar", lambda e: e.copy(out=grsb[:], in_=pt16[:]),
              wait=[("tensor", mm_t)])
    p.dma("sync", grow_row[0:1, :], grsb[:], wait=[("scalar", ev)])
    p.barrier()
    ar.close()


def bass_diag36(t):
    ap = t[:]
    return bass.AP(ap.tensor, ap.offset,
                   [list(ap.ap[0]), [36, 16], [7, 6]])


def _attn_out(k, b, seq_sc, memv_sc, grow_row, wout_name, bout_name, y_sc):
    p = k.p
    onr = k.c["ones_row_bf"]
    ar = Arena(k.nc, p)
    seqb = ar.sb("ao_seq", [128, 8 * TB], BF16)
    ld = _load_inb(k, seqb, [seq_sc.rearrange("(c p) T -> p c T", p=128)
                             [:, :, b * TB:(b + 1) * TB]])
    if DBG_SUMS2[0] is not None:
        p.op("vector", lambda e: e.memset(
            _ln_scr[0][:, 0:1] if False else seqb[0:1, 0:1], 0.0),
            wait=list(ld)) if False else None
        p.dma("sync", DBG_SUMS2[0][b], seqb[:].rearrange(
            "p (c t) -> p c t", c=8), wait=list(ld))
        p.barrier()
    if memv_sc is not None:
        car = Arena(k.nc, p)
        mvb = car.sb("ao_mv", [128, 8 * TB], BF16)
        tmp = car.sb("ao_tmp", [128, 512])
        g_bcs = [car.sb("ao_gbc", [128, 512]) for _ in range(4)]
        psRs = [car.ps("ao_psR", [128, 512]) for _ in range(2)]
        ld2 = _load_inb(k, mvb, [memv_sc.rearrange("(c p) T -> p c T", p=128)
                                 [:, :, b * TB:(b + 1) * TB]])
        rfree = {}
        for tc in range(4):
            g_bc = g_bcs[tc]
            psR = psRs[tc % 2]
            w = list(ld) + list(ld2)
            if (tc % 2) in rfree:
                w.append(rfree[tc % 2])
            mmr = p.op("tensor", lambda e, tc=tc, psR=psR: e.matmul(
                out=psR[:], lhsT=onr[0:1, :],
                rhs=grow_row[0:1, tc * 512:(tc + 1) * 512],
                start=True, stop=True), wait=w)
            evr = p.op("scalar", lambda e, g_bc=g_bc, psR=psR: e.copy(
                out=g_bc[:], in_=psR[:]), wait=[("tensor", mmr)])
            rfree[tc % 2] = ("scalar", evr)
            if DBG_GBC[0] is not None:
                p.barrier()
                p.dma("sync", DBG_GBC[0][b, tc], g_bc[:])
                p.barrier()
            for dc in range(8):
                p.op("vector", lambda e, dc=dc, tc=tc, g_bc=g_bc: e.tensor_mul(
                    tmp[:], mvb[:, dc * TB + tc * 512: dc * TB + tc * 512 + 512],
                    g_bc[:]), wait=[("scalar", evr)])
                p.op("vector", lambda e, dc=dc, tc=tc: e.tensor_add(
                    seqb[:, dc * TB + tc * 512: dc * TB + tc * 512 + 512],
                    tmp[:],
                    seqb[:, dc * TB + tc * 512: dc * TB + tc * 512 + 512]))
        if DBG_MVB[0] is not None:
            p.dma("sync", DBG_MVB[0][b], mvb[:].rearrange(
                "p (c t) -> p c t", c=8), wait=[("vector", p.n("vector"))])
        p.barrier()
        if DBG_SUMS[0] is not None:
            p.dma("sync", DBG_SUMS[0][b], seqb[:].rearrange(
                "p (c t) -> p c t", c=8))
            p.barrier()
        car.close()
    wt = [ar.sb("ao_w", [128, 8 * 128], BF16) for _ in range(2)]
    stage = [ar.sb("ao_stg", [128, TB], BF16) for _ in range(2)]
    ps = [ar.ps("ao_ps", [128, 512]) for _ in range(8)]
    bufs = (seqb, wt, None, stage, ps)
    _fm_chain_layer(k, ar, bufs, wout_name, bout_name, 8, list(range(8)),
                    lambda c, b=b: _fm_rows(y_sc, c, b), "bias",
                    list(ld) + [("vector", p.n("vector"))])
    ar.close()


def ph_l0attn(k, xnT, memv_sc, seq_sc, y_sc, hook=None):
    p = k.p
    oar = Arena(k.nc, p)
    wproj_sb = oar.sb("la_wproj", [128, 8 * 272], BF16)
    bq_sc = oar.sb("la_bq", [128, 8])
    grow_row = oar.sb("la_grow", [1, TB], BF16)
    p.dma("gpsimd", wproj_sb[:].rearrange("p (kc x) -> p kc x", kc=8),
          k.t["w_l0proj"].rearrange("(kc p) x -> p kc x", p=128))
    p.op("vector", lambda e: e.tensor_scalar_mul(
        bq_sc[:], k.c["b_qkv_col"][:, 0:8], 0.125),
        wait=[("sync", p.n("sync"))])
    p.barrier()
    for b in range(NB):
        ar = Arena(k.nc, p)
        inb = ar.sb("la_in", [128, 8 * (TB + 1)], BF16)
        qsb = ar.sb("la_q", [128, 8 * TB], BF16)
        ksb = ar.sb("la_k", [128, 8 * TB], BF16)
        vsb = ar.sb("la_v", [128, 16 * 1024], BF16)
        p.dma("sync", inb[:].rearrange("p (c t) -> p c t", c=8),
              xnT[b].rearrange("(c p) T -> p c T", p=128))
        mar = Arena(k.nc, p)
        mwt = [mar.sb("la_mw", [128, 8 * 128], BF16) for _ in range(2)]
        mstage = [mar.sb("la_mst", [128, TB], BF16) for _ in range(2)]
        mps = [mar.ps("la_mps", [128, 512]) for _ in range(8)]
        _fm_chain_layer(k, mar, (inb, mwt, None, mstage, mps), "w_memv",
                        "b_memv_col", 8, list(range(8)),
                        lambda c, b=b: _fm_rows(memv_sc, c, b), "bias",
                        [("sync", p.n("sync"))], in_stride=TB + 1, in_off=1)
        mar.close()
        _attn_proj(k, b, ar, inb, qsb, ksb, vsb, "w_qkv", "w_qkv_v",
                   k.c["b_qkv_v_row"][0:1, :], bq_sc, k.c["b_qkv_col"][:, 8:16])
        ph_attn_core(k, 0, b, ar, inb, qsb, ksb, vsb, seq_sc)
        _mem_path(k, b, inb, wproj_sb, grow_row[0:1, :])
        if DBG_GROW[0] is not None:
            p.barrier()
            p.dma("sync", DBG_GROW[0][b], grow_row[0:1, :])
            p.barrier()
        ar.close()
        _attn_out(k, b, seq_sc, memv_sc, grow_row[0:1, :], "w_l0out",
                  "b_l0out_col", y_sc)
        if hook is not None:
            hook(b)
    p.barrier()
    oar.close()


def ph_l1attn(k, xnT, seq_sc, y_sc):
    p = k.p
    oar = Arena(k.nc, p)
    p.barrier()
    for b in range(NB):
        ar = Arena(k.nc, p)
        inb = ar.sb("lb_in", [128, 8 * (TB + 1)], BF16)
        qsb = ar.sb("lb_q", [128, 8 * TB], BF16)
        ksb = ar.sb("lb_k", [128, 8 * TB], BF16)
        vsb = ar.sb("lb_v", [128, 16 * 1024], BF16)
        p.dma("sync", inb[:].rearrange("p (c t) -> p c t", c=8),
              xnT[b].rearrange("(c p) T -> p c T", p=128))
        _attn_proj(k, b, ar, inb, qsb, ksb, vsb, "w_l1in", "w_l1in_v",
                   None, None, None)
        ph_attn_core(k, 1, b, ar, inb, qsb, ksb, vsb, seq_sc)
        ar.close()
        _attn_out(k, b, seq_sc, None, None, "w_l1out", None, y_sc)
    p.barrier()
    oar.close()


L0ATTN_W = ["w_qkv", "b_qkv_col", "w_qkv_v", "b_qkv_v_row", "w_l0proj",
            "b_memg_tile", "memscale_tile", "w_memv", "b_memv_col",
            "w_l0out", "b_l0out_col"]
L1ATTN_W = ["w_l1in", "w_l1in_v", "w_l1out"]


def build_l0attn_launch(host, dbg=False):
    nc = bass.Bass("TRN2", target_bir_lowering=False, debug=False)
    k = K(nc, host)
    kd = "ExternalOutput" if dbg else "Internal"
    x_in = k.dram("x_in", [TOK, D], F32, kind="ExternalInput")
    x_out = k.dram("x_out", [TOK, D], F32, kind="ExternalOutput")
    k.declare_weights([w for w in L0ATTN_W if w not in
                       ("b_qkv_v_row",)])
    xnT = [k.dram(f"xnT{b}", [D, TB + 1], BF16, kind=kd) for b in range(NB)]
    memv_sc = k.dram("memv_sc", [D, TOK], BF16, kind=kd)
    seq_sc = k.dram("seq_sc", [D, TOK], BF16, kind=kd)
    y_sc = k.dram("y_sc", [D, TOK], BF16, kind=kd)
    ph_consts(k, ["ident_bf", "ident_f32", "ones_row_bf", "cmask", "cmaskT",
                  "E5", "E5T", "pfx_bf", "wdn_col", "gcol_bf", "tloc_col",
                  "b_qkv_col", "b_memv_col", "b_l0out_col", "b_memg_tile",
                  "memscale_tile", "b_qkv_v_row"])
    ph_preln(k, x_in, xnT)
    ph_l0attn(k, xnT, memv_sc, seq_sc, y_sc)
    ph_post(k, y_sc, False, x_in, x_out)
    k.p.finalize()
    return k


def build_l1attn_launch(host, dbg=False):
    nc = bass.Bass("TRN2", target_bir_lowering=False, debug=False)
    k = K(nc, host)
    kd = "ExternalOutput" if dbg else "Internal"
    x_in = k.dram("x_in", [TOK, D], F32, kind="ExternalInput")
    x_out = k.dram("x_out", [TOK, D], F32, kind="ExternalOutput")
    k.declare_weights(L1ATTN_W[0:1] + L1ATTN_W[2:3])
    xnT = [k.dram(f"xnT{b}", [D, TB + 1], BF16, kind=kd) for b in range(NB)]
    seq_sc = k.dram("seq_sc", [D, TOK], BF16, kind=kd)
    y_sc = k.dram("y_sc", [D, TOK], BF16, kind=kd)
    ph_consts(k, ["ident_bf", "ident_f32", "ones_row_bf", "cmask", "cmaskT",
                  "E5", "E5T"])
    k.dram("w_l1in_v", host["w_l1in_v"].shape, F32, kind="ExternalInput")
    ph_preln(k, x_in, xnT)
    ph_l1attn(k, xnT, seq_sc, y_sc)
    ph_post(k, y_sc, False, x_in, x_out)
    k.p.finalize()
    return k


def kernel_unfused(**inputs):
    host = prep_host(inputs)
    xs = _shard_x(inputs)
    cores = list(range(NCORES))

    k1 = build_l0attn_launch(host)
    res = run_bass_kernel_spmd(
        k1.nc, [k1.in_map({"x_in": xs[c]}) for c in cores], core_ids=cores)
    xs = [np.asarray(res.results[c]["x_out"], np.float32) for c in cores]

    k2, _ = build_l0ffn_launch(host)
    res = run_bass_kernel_spmd(
        k2.nc, [k2.in_map({"x_in": xs[c]}) for c in cores], core_ids=cores)
    xs = [np.asarray(res.results[c]["x_out"], np.float32) for c in cores]

    k3 = build_l1attn_launch(host)
    res = run_bass_kernel_spmd(
        k3.nc, [k3.in_map({"x_in": xs[c]}) for c in cores], core_ids=cores)
    xs = [np.asarray(res.results[c]["x_out"], np.float32) for c in cores]

    k4 = build_moe_launch(host)
    res = run_bass_kernel_spmd(
        k4.nc, [k4.in_map({"x_in": xs[c]}) for c in cores], core_ids=cores)
    xs = [np.asarray(res.results[c]["x_out"], np.float32) for c in cores]

    out = np.stack([x.reshape(NB, TB, D) for x in xs])
    return np.ascontiguousarray(out.reshape(16, TB, D))


def build_fused(host):
    nc = bass.Bass("TRN2", target_bir_lowering=False, debug=False)
    k = K(nc, host)
    x_in = k.dram("x_in", [TOK, D], F32, kind="ExternalInput")
    x_out = k.dram("x_out", [TOK, D], F32, kind="ExternalOutput")
    k.declare_weights([w for w in L0ATTN_W if w != "b_qkv_v_row"])
    k.declare_weights(L0FFN_W)
    k.declare_weights(["w_l1in", "w_l1out"])
    k.declare_weights(MOE_W)
    k.dram("w_l1in_v", host["w_l1in_v"].shape, F32, kind="ExternalInput")
    x_res = k.dram("x_res", [TOK, D], F32)
    xnT = [k.dram(f"xnT{b}", [D, TB + 1], BF16) for b in range(NB)]
    memv_sc = k.dram("memv_sc", [D, TOK], BF16)
    seq_sc = k.dram("seq_sc", [D, TOK], BF16)
    y_sc = k.dram("y_sc", [D, TOK], BF16)
    y32 = k.dram("y32", [D, TOK], F32)
    h_sc = k.dram("h_sc", [4096, TOK], BF16)
    h2_sc = k.dram("h2_sc", [4096, TOK], BF16)
    h3_sc = k.dram("h3_sc", [4096, TOK], BF16)
    for mn in ("we1b", "we3b", "we2b"):
        k.t[mn] = k.dram(mn, [32, 128, 4096], BF16, kind="Internal")
    ph_consts(k, ["ident_bf", "ident_f32", "ones_row_bf", "cmask", "cmaskT",
                  "E5", "E5T", "pfx_bf", "wdn_col", "gcol_bf", "tloc_col",
                  "b_qkv_col", "b_memv_col", "b_l0out_col", "b_memg_tile",
                  "memscale_tile", "b_qkv_v_row",
                  "s_f12_col", "s_fh1_col", "s_fh2_col", "s_f3_col"])
    k.p.phase = "preln0"
    ph_preln(k, x_in, xnT)
    k.p.phase = "l0attn"
    ph_l0attn(k, xnT, memv_sc, seq_sc, y_sc,
              hook=lambda b: ph_precast_experts(k, 16 * b, 16 * (b + 1)))
    k.p.phase = "post0"
    ph_post(k, y_sc, False, x_in, x_res)
    k.p.phase = "preln1"
    ph_preln(k, x_res, xnT)
    k.p.phase = "l0ffn"
    ph_l0ffn(k, xnT, h_sc, h2_sc, h3_sc, y_sc)
    k.p.phase = "post1"
    ph_post(k, y_sc, False, x_res, x_res)
    k.p.phase = "preln2"
    ph_preln(k, x_res, xnT)
    k.p.phase = "l1attn"
    ph_l1attn(k, xnT, seq_sc, y_sc)
    k.p.phase = "post2"
    ph_post(k, y_sc, False, x_res, x_res)
    k.p.phase = "preln3"
    ph_preln(k, x_res, xnT)
    k.p.phase = "moe"
    ph_moe(k, xnT, y32)
    k.p.phase = "post3"
    ph_post(k, y32, True, x_res, x_out)
    k.p.finalize()
    return k


def kernel(**inputs):
    host = prep_host(inputs)
    xs = _shard_x(inputs)
    cores = list(range(NCORES))
    kk = build_fused(host)
    res = run_bass_kernel_spmd(
        kk.nc, [kk.in_map({"x_in": xs[c]}) for c in cores], core_ids=cores)
    xs = [np.asarray(res.results[c]["x_out"], np.float32) for c in cores]
    out = np.stack([x.reshape(NB, TB, D) for x in xs])
    return np.ascontiguousarray(out.reshape(16, TB, D))
